# version 15 (final)
# baseline (speedup 1.0000x reference)
import contextlib
import numpy as np
import concourse.bass as bass
import concourse.mybir as mybir
from concourse.bass_utils import run_bass_kernel_spmd

F32 = mybir.dt.float32
BF16 = mybir.dt.bfloat16
AF = mybir.ActivationFunctionType
ALU = mybir.AluOpType
AX = mybir.AxisListType

NCORES = 8
L = 16384
HID = 1024
NH = 8
T = L // NCORES
NSC = 4
NCH = 16
SCALE = float(1.0 / np.sqrt(128.0))
ENGS = ("sync", "vector", "scalar", "tensor", "gpsimd")

AFC = 21504
ABC = 40960


def build_program(debug=False):
    nc = bass.Bass("TRN2", target_bir_lowering=False, debug=False)

    x_in = nc.dram_tensor("x", [T, HID], F32, kind="ExternalInput").ap()
    wq_in = nc.dram_tensor("wq", [8, 128, 1024], F32, kind="ExternalInput").ap()
    wk_in = nc.dram_tensor("wk", [8, 128, 1024], F32, kind="ExternalInput").ap()
    wv_in = nc.dram_tensor("wv", [128, 8192], F32, kind="ExternalInput").ap()
    wo_in = nc.dram_tensor("wo", [128, 8192], F32, kind="ExternalInput").ap()
    we_in = nc.dram_tensor("we", [8, 128, 8192], F32, kind="ExternalInput").ap()
    wg_in = nc.dram_tensor("wg", [128, 192], F32, kind="ExternalInput").ap()
    gw_in = nc.dram_tensor("gw", [128, 64], F32, kind="ExternalInput").ap()
    bg_in = nc.dram_tensor("bg", [24, 1], F32, kind="ExternalInput").ap()
    boutbc_in = nc.dram_tensor("boutbc", [128, HID], F32, kind="ExternalInput").ap()
    nwc_in = nc.dram_tensor("nwc", [128, 8], F32, kind="ExternalInput").ap()
    cos_in = nc.dram_tensor("cosr", [128, T], F32, kind="ExternalInput").ap()
    sin_in = nc.dram_tensor("sinr", [128, T], F32, kind="ExternalInput").ap()
    tri_in = nc.dram_tensor("tri", [128, 128], F32, kind="ExternalInput").ap()
    ident_in = nc.dram_tensor("ident", [128, 128], F32, kind="ExternalInput").ap()
    ones1_in = nc.dram_tensor("ones1", [1, 128], F32, kind="ExternalInput").ap()
    sel_in = nc.dram_tensor("selc", [8, 1], F32, kind="ExternalInput").ap()
    onems_in = nc.dram_tensor("onems", [8, 8], F32, kind="ExternalInput").ap()
    selrow_in = nc.dram_tensor("selrow", [1, 8], F32, kind="ExternalInput").ap()
    epsc_in = nc.dram_tensor("epsc", [128, 4], F32, kind="ExternalInput").ap()
    esel_in = nc.dram_tensor("esel", [16, 2048], F32, kind="ExternalInput").ap()

    out_ext = nc.dram_tensor("out", [T, HID], F32, kind="ExternalOutput").ap()

    xnth_d = nc.dram_tensor("xnth_d", [NSC, 128, 4096], BF16).ap()
    xntl_d = nc.dram_tensor("xntl_d", [NSC, 128, 4096], BF16).ap()
    kt_d = nc.dram_tensor("kt_d", [NSC, 128, 4096], F32).ap()
    v_d = nc.dram_tensor("v_d", [NSC, 128, 4096], F32).ap()
    u_d = nc.dram_tensor("u_d", [NCH, 128, 1024], F32).ap()
    weh_d = nc.dram_tensor("weh_d", [8, 128, 8192], BF16).ap()
    wel_d = nc.dram_tensor("wel_d", [8, 128, 8192], BF16).ap()
    yth_d = nc.dram_tensor("yth_d", [NSC, 128, 4096], BF16).ap()
    ytl_d = nc.dram_tensor("ytl_d", [NSC, 128, 4096], BF16).ap()
    mex_i = nc.dram_tensor("mex_i", [128, 1024], F32).ap()
    mex_o = nc.dram_tensor("mex_o", [1024, 1024], F32).ap()
    aseg_i = nc.dram_tensor("aseg_i", [1, 8], F32).ap()
    aseg_o = nc.dram_tensor("aseg_o", [8, 8], F32).ap()

    dbg = {}
    if debug:
        for name, shape in [("d_min", [128, 1024]), ("d_mloc", [128, 1024]),
                            ("d_cols", [128, 384]), ("d_abcol", [128, 128]),
                            ("d_aends", [8, 16]), ("d_stats", [128, 48]),
                            ("d_y", [128, 4096]), ("d_onehot", [128, 128]),
                            ("d_ASC", [128, 4096]), ("d_gl", [24, 512]),
                            ("d_zl", [24, 512]), ("d_al", [8, 512]),
                            ("d_sb", [128, 512]), ("d_xnh", [128, 512]),
                            ("d_xnl", [128, 512]), ("d_st0", [128, 1]), ("d_st1", [128, 1]),
                            ("d_st2", [128, 1]), ("d_st3", [128, 1])]:
            dbg[name] = nc.dram_tensor(name, shape, F32, kind="ExternalOutput").ap()

    af_h = nc.alloc_sbuf_tensor("arena_f", [128, AFC], F32)
    ab_h = nc.alloc_sbuf_tensor("arena_b", [128, ABC], BF16)

    def AFV(off, wdt, parts=128, p0=0):
        return af_h[p0:p0 + parts, off:off + wdt]

    def ABV(off, wdt, parts=128, p0=0):
        return ab_h[p0:p0 + parts, off:off + wdt]

    sbt = lambda n, s, d=F32: nc.alloc_sbuf_tensor(n, s, d)
    cols = sbt("cols", [128, NCH * 24])
    abcol = sbt("abcol", [128, NCH * 8])
    aends = sbt("aends", [8, NCH])
    ascan = sbt("ascan", [8, NCH])
    aendT = sbt("aendT", [NCH, 8])
    asegrow = sbt("asegrow", [1, 8])
    asegall = sbt("asegall", [8, 8])
    alpharows = sbt("alpharows", [8, 8])
    albc = sbt("albc", [128, 64])
    selbc = sbt("selbc", [128, 8])
    selsb = sbt("selsb", [8, 1])
    onemsb = sbt("onemsb", [8, 8])
    selrowsb = sbt("selrowsb", [1, 8])
    M_sb = sbt("M_sb", [128, 1024])
    Mloc = sbt("Mloc", [128, 1024])
    Mterm = sbt("Mterm", [128, 128])
    P_sb = sbt("P_sb", [128, 128])
    P_sb2 = sbt("P_sb2", [128, 128])
    sbcast = sbt("sbcast", [128, 512])
    ones1 = sbt("ones1_sb", [1, 128])
    ident = sbt("ident_sb", [128, 128])
    tri = sbt("tri_sb", [128, 128])
    zeros8 = sbt("zeros8", [8, 128])
    zero128 = sbt("zero128", [128, 8])
    rstd_a = sbt("rstd_a", [128, NCH])
    nmr_a = sbt("nmr_a", [128, NCH])
    s_a = sbt("s_a", [128, NCH])
    skT = sbt("skT", [NCH, 128])
    mcol = sbt("mcol", [128, 1])
    mc4 = sbt("mc4", [128, 4])
    ng4 = sbt("ng4", [128, 4])
    c14 = sbt("c14", [128, 4])
    c24 = sbt("c24", [128, 4])
    negm = sbt("negm", [128, 1])
    c1 = sbt("c1", [128, 1])
    c2 = sbt("c2", [128, 1])
    nwcols = sbt("nwcols", [128, 8])
    bgcol = sbt("bgcol", [24, 1])
    wgstage = sbt("wgstage", [128, 192])
    wghi = sbt("wghi", [128, 192], BF16)
    wglo = sbt("wglo", [128, 192], BF16)
    gwstage = sbt("gwstage", [128, 64])
    gwhi = sbt("gwhi", [128, 64], BF16)
    gwlo = sbt("gwlo", [128, 64], BF16)
    onehot = sbt("onehot", [128, NCH * 8])
    scores8 = sbt("scores8", [128, 32])
    sc8a = sbt("sc8a", [128, 8])
    sc8b = sbt("sc8b", [128, 8])
    sc8c = sbt("sc8c", [128, 8])
    mx1 = sbt("mx1", [128, 1])
    epsc = sbt("epsc_sb", [128, 4])
    bnop = sbt("bnop", [128, 4])
    esel = sbt("esel_sb", [16, 2048])
    cossb = sbt("cossb", [128, 512])
    sinsb = sbt("sinsb", [128, 512])

    pb = [nc.alloc_psum_tensor(f"pb{i}", [128, 512], F32) for i in range(8)]
    assert nc.sbuf_bytes_remaining >= 0, f"SBUF over: {nc.sbuf_bytes_remaining}"

    plan = {e: [] for e in ENGS}
    stack = contextlib.ExitStack()
    sems, cnt = {}, {}
    for name in ("bar", "t", "v", "s", "g", "cc"):
        sems[name] = stack.enter_context(nc.semaphore(name))
        cnt[name] = 0
    QR = 4
    qsems, qcnt, qrr = {}, {}, {}
    for name in ("din", "dg", "dst", "o"):
        qsems[name] = [stack.enter_context(nc.semaphore(f"{name}{i}"))
                       for i in range(QR)]
        qcnt[name] = [0] * QR
        qrr[name] = 0

    def emit(eng, fn):
        plan[eng].append(fn)

    def dtok(name):
        return dict(enumerate(qcnt[name]))

    def w(eng, sem, val=None):
        if sem in qsems:
            tok = val if isinstance(val, dict) else dtok(sem)
            for i, v2 in tok.items():
                if v2 > 0:
                    h = qsems[sem][i]
                    emit(eng, lambda e, h=h, v3=v2: e.wait_ge(h, v3))
            return
        val = cnt[sem] if val is None else val
        if val <= 0:
            return
        h = sems[sem]
        emit(eng, lambda e, h=h, v2=val: e.wait_ge(h, v2))

    def inc(eng, sem, by, fn):
        cnt[sem] += by
        h = sems[sem]
        emit(eng, lambda e, h=h, by=by, fn=fn: fn(e).then_inc(h, by))

    def run(eng, fn):
        if eng in OWN:
            inc(eng, OWN[eng], 1, fn)
        else:
            emit(eng, fn)

    def snap():
        return dict(t=cnt["t"], v=cnt["v"], s=cnt["s"], g=cnt["g"])

    def wsnap(eng, sn):
        for k2 in ("t", "v", "s", "g"):
            if sn[k2] > 0:
                w(eng, k2, sn[k2])

    OWN = {"vector": "v", "scalar": "s", "tensor": "t", "gpsimd": "g"}

    def barrier():
        for _ in ENGS:
            cnt["bar"] += 1
        tgt = cnt["bar"]
        h = sems["bar"]
        for eng in ENGS:
            if eng in OWN:
                w(eng, OWN[eng])
            emit(eng, lambda e, h=h: e.sem_inc(h, 1))
            emit(eng, lambda e, h=h, t2=tgt: e.wait_ge(h, t2))

    def dma(eng, sem, out, in_):
        i = qrr[sem]
        qrr[sem] = (i + 1) % QR
        h = qsems[sem][i]
        prev = qcnt[sem][i]
        if prev > 0:
            emit(eng, lambda e, h=h, v2=prev: e.wait_ge(h, v2))
        qcnt[sem][i] = prev + 16
        emit(eng, lambda e, h=h, o=out, i2=in_: e.dma_start(out=o, in_=i2).then_inc(h, 16))

    bank_free = {i: None for i in range(8)}

    def bank_wait(eng, i):
        if bank_free[i] is not None:
            sname, val = bank_free[i]
            w(eng, sname, val)

    ssw_state = {"vdone": {}}

    def stream_split_stationary(win, stage_off, hi_off=8192, lo_off=16384):
        ssw_state["vdone"] = {}
        for mb in range(8):
            slot = AFV(stage_off + (mb % 2) * 1024, 1024)
            if mb >= 2:
                w("sync", "v", ssw_state["vdone"][mb - 2])
            dma("sync", "din", slot, win[mb])
            w("vector", "din")
            for kc in range(8):
                hi = ABV(hi_off + mb * 1024 + kc * 128, 128)
                st = slot[:, kc * 128:(kc + 1) * 128]
                fn = lambda e, hi=hi, st=st, kc=kc: e.tensor_scalar(
                    hi, st, nwcols[:, kc:kc + 1], None, ALU.mult)
                inc("vector", "v", 1, fn)
            w("vector", "v")
            for kc in range(8):
                hi = ABV(hi_off + mb * 1024 + kc * 128, 128)
                lo = ABV(lo_off + mb * 1024 + kc * 128, 128)
                st = slot[:, kc * 128:(kc + 1) * 128]
                fn = lambda e, hi=hi, lo=lo, st=st, kc=kc: e.scalar_tensor_tensor(
                    lo, st, nwcols[:, kc:kc + 1], hi, ALU.mult, ALU.subtract)
                inc("vector", "v", 1, fn)
            ssw_state["vdone"][mb] = cnt["v"]

    def stream_split_moving(win, fold_norm, stage_off):
        for hf in range(2):
            slot = AFV(stage_off, 4096)
            if hf == 1:
                w("sync", "v")
            dma("sync", "din", slot, win[:, hf * 4096:(hf + 1) * 4096])
            w("vector", "din")
            for kc in range(4):
                kca = hf * 4 + kc
                hi = ABV(8192 + hf * 4096 + kc * 1024, 1024)
                st = slot[:, kc * 1024:(kc + 1) * 1024]
                if fold_norm:
                    fn = lambda e, hi=hi, st=st, kca=kca: e.tensor_scalar(
                        hi, st, nwcols[:, kca:kca + 1], None, ALU.mult)
                else:
                    fn = lambda e, hi=hi, st=st: e.tensor_copy(hi, st)
                inc("vector", "v", 1, fn)
            w("vector", "v")
            for kc in range(4):
                kca = hf * 4 + kc
                hi = ABV(8192 + hf * 4096 + kc * 1024, 1024)
                lo = ABV(16384 + hf * 4096 + kc * 1024, 1024)
                st = slot[:, kc * 1024:(kc + 1) * 1024]
                if fold_norm:
                    fn = lambda e, hi=hi, lo=lo, st=st, kca=kca: e.scalar_tensor_tensor(
                        lo, st, nwcols[:, kca:kca + 1], hi, ALU.mult, ALU.subtract)
                else:
                    fn = lambda e, hi=hi, lo=lo, st=st: e.tensor_tensor(
                        lo, st, hi, ALU.subtract)
                inc("vector", "v", 1, fn)

    def build_sbcast(sc):
        bank_wait("tensor", 6)

        def mk(e, sc=sc):
            last = None
            for wj in range(4):
                tc = sc * 4 + wj
                last = e.matmul(pb[6][:, wj * 128:(wj + 1) * 128],
                                esel[:, tc * 128:(tc + 1) * 128],
                                skT[0:16, 0:128], start=True, stop=True)
            return last
        inc("tensor", "t", 1, mk)
        w("scalar", "t")
        w("scalar", "v")
        inc("scalar", "s", 1, lambda e: e.activation(sbcast[:, :], pb[6][:, 0:512], AF.Copy))
        bank_free[6] = ("s", cnt["s"])

    def proj_stationary_weight(dst_base, xbase=(0, 4096), wbase=(8192, 16384)):
        for mb in range(8):
            bi = mb % 2
            bank = pb[bi]
            bank_wait("tensor", bi)

            def mk(e, mb=mb, bank=bank):
                last = None
                for kc in range(8):
                    xh = ABV(xbase[0] + kc * 512, 512)
                    xl = ABV(xbase[1] + kc * 512, 512)
                    wh = ABV(wbase[0] + mb * 1024 + kc * 128, 128)
                    wl = ABV(wbase[1] + mb * 1024 + kc * 128, 128)
                    e.matmul(bank[:, 0:512], wh, xh, start=(kc == 0), stop=False)
                    e.matmul(bank[:, 0:512], wh, xl, start=False, stop=False)
                    last = e.matmul(bank[:, 0:512], wl, xh, start=False, stop=(kc == 7))
                return last
            inc("tensor", "t", 1, mk)
            w("vector", "t")
            w("vector", "s")
            dst = AFV(dst_base + mb * 512, 512)
            inc("vector", "v", 1, lambda e, dst=dst, bank=bank: e.tensor_tensor(
                dst, bank[:, 0:512], sbcast[:, :], ALU.mult))
            bank_free[bi] = ("v", cnt["v"])

    def rope(base, scrA, scrB, eng2="gpsimd"):
        x1 = AFV(base, 2048)
        x2 = AFV(base + 2048, 2048)
        tA = AFV(scrA, 2048)
        tB = AFV(scrB, 2048)
        cosv = cossb.ap()[:, 0:512].unsqueeze(1).broadcast_to([128, 4, 512])
        sinv = sinsb.ap()[:, 0:512].unsqueeze(1).broadcast_to([128, 4, 512])

        def v3(apx):
            return apx.rearrange("p (b f) -> p b f", b=4)
        w("vector", "din")
        run("vector", lambda e: e.tensor_tensor(v3(tA), v3(x1), sinv, ALU.mult))
        inc("vector", "v", 1, lambda e: e.tensor_tensor(v3(x1), v3(x1), cosv, ALU.mult))
        if eng2 == "gpsimd":
            w("gpsimd", "din")
            inc("gpsimd", "g", 1, lambda e: e.tensor_tensor(v3(tB), v3(x2), sinv, ALU.mult))
            inc("gpsimd", "g", 1, lambda e: e.tensor_tensor(v3(x2), v3(x2), cosv, ALU.mult))
            w("vector", "g")
            w("vector", "v")
        else:
            run("vector", lambda e: e.tensor_tensor(v3(tB), v3(x2), sinv, ALU.mult))
            inc("vector", "v", 1, lambda e: e.tensor_tensor(v3(x2), v3(x2), cosv, ALU.mult))
            w("vector", "v")
        run("vector", lambda e: e.tensor_tensor(v3(x1), v3(x1), v3(tB), ALU.subtract))
        inc("vector", "v", 1, lambda e: e.tensor_tensor(v3(x2), v3(x2), v3(tA), ALU.add))


    dma("sync", "din", tri[:, :], tri_in)
    dma("sync", "din", ident[:, :], ident_in)
    dma("sync", "din", ones1[:, :], ones1_in)
    dma("sync", "din", nwcols[:, :], nwc_in)
    dma("sync", "din", bgcol[:, :], bg_in)
    dma("sync", "din", selsb[:, :], sel_in)
    dma("sync", "din", onemsb[:, :], onems_in)
    dma("sync", "din", selrowsb[:, :], selrow_in)
    dma("sync", "din", wgstage[:, :], wg_in)
    dma("sync", "din", gwstage[:, :], gw_in)
    dma("sync", "din", epsc[:, :], epsc_in)
    dma("sync", "din", esel[:, :], esel_in)
    run("vector", lambda e: e.memset(zeros8[:, :], 0.0))
    run("vector", lambda e: e.memset(zero128[:, :], 0.0))
    run("vector", lambda e: e.memset(M_sb[:, :], 0.0))
    inc("vector", "v", 1, lambda e: e.memset(Mloc[:, :], 0.0))
    w("gpsimd", "din")
    for kc in range(8):
        inc("gpsimd", "g", 1, lambda e, kc=kc: e.tensor_scalar(
            wgstage[:, kc * 24:(kc + 1) * 24], wgstage[:, kc * 24:(kc + 1) * 24],
            nwcols[:, kc:kc + 1], None, ALU.mult))
    w("gpsimd", "g")
    inc("gpsimd", "g", 1, lambda e: e.tensor_copy(wghi[:, :], wgstage[:, :]))
    w("gpsimd", "g")
    inc("gpsimd", "g", 1, lambda e: e.tensor_tensor(wglo[:, :], wgstage[:, :], wghi[:, :], ALU.subtract))
    inc("gpsimd", "g", 1, lambda e: e.tensor_copy(gwhi[:, :], gwstage[:, :]))
    w("gpsimd", "g")
    inc("gpsimd", "g", 1, lambda e: e.tensor_tensor(gwlo[:, :], gwstage[:, :], gwhi[:, :], ALU.subtract))

    stream_split_moving(wv_in, True, 10240)
    wv_done = snap()

    for sc in range(NSC):
        for wj in range(4):
            tc = sc * 4 + wj
            dma("sync", "din", AFV(wj * 1024, 1024), x_in[tc * 128:(tc + 1) * 128, :])
        w("vector", "din")
        for wj in range(4):
            fn = lambda e, wj=wj: e.reduce_sum(mc4[:, wj:wj + 1], AFV(wj * 1024, 1024), axis=AX.X)
            inc("vector", "v", 1, fn)
        w("vector", "v")
        for wj in range(4):
            fn = lambda e, wj=wj: e.tensor_scalar(
                ng4[:, wj:wj + 1], mc4[:, wj:wj + 1], -1.0 / 1024.0, None, ALU.mult)
            inc("vector", "v", 1, fn)
        w("scalar", "v")
        for wj in range(4):
            fn = lambda e, wj=wj: e.activation(
                AFV(16384 + wj * 1024, 1024), AFV(wj * 1024, 1024), AF.Square,
                bias=ng4[:, wj:wj + 1], scale=1.0, accum_out=c14[:, wj:wj + 1])
            inc("scalar", "s", 1, fn)
        w("scalar", "s")
        for wj in range(4):
            fn = lambda e, wj=wj: e.activation(
                c24[:, wj:wj + 1], c14[:, wj:wj + 1], AF.Sqrt,
                bias=epsc[:, 1:2], scale=epsc[:, 0:1])
            inc("scalar", "s", 1, fn)
        w("vector", "s")
        for wj in range(4):
            tc = sc * 4 + wj
            fn = lambda e, wj=wj, tc=tc: e.reciprocal(rstd_a[:, tc:tc + 1], c24[:, wj:wj + 1])
            inc("vector", "v", 1, fn)
        w("vector", "v")
        for wj in range(4):
            tc = sc * 4 + wj
            fn = lambda e, wj=wj, tc=tc: e.tensor_mul(
                nmr_a[:, tc:tc + 1], ng4[:, wj:wj + 1], rstd_a[:, tc:tc + 1])
            inc("vector", "v", 1, fn)
        w("vector", "v")
        for wj in range(4):
            tc = sc * 4 + wj
            fn = lambda e, wj=wj, tc=tc: e.tensor_scalar(
                AFV(5120 + wj * 1024, 1024), AFV(wj * 1024, 1024),
                rstd_a[:, tc:tc + 1], nmr_a[:, tc:tc + 1], ALU.mult, ALU.add)
            inc("vector", "v", 1, fn)
        w("scalar", "v")
        for wj in range(4):
            fn = lambda e, wj=wj: e.activation(
                AFV(16384 + wj * 1024, 1024), AFV(5120 + wj * 1024, 1024), AF.Square,
                accum_out=c14[:, wj:wj + 1])
            inc("scalar", "s", 1, fn)
        w("scalar", "s")
        for wj in range(4):
            fn = lambda e, wj=wj: e.activation(
                c24[:, wj:wj + 1], c14[:, wj:wj + 1], AF.Sqrt,
                bias=epsc[:, 2:3], scale=epsc[:, 0:1])
            inc("scalar", "s", 1, fn)
        w("vector", "s")
        for wj in range(4):
            tc = sc * 4 + wj
            fn = lambda e, wj=wj, tc=tc: e.reciprocal(s_a[:, tc:tc + 1], c24[:, wj:wj + 1])
            inc("vector", "v", 1, fn)
        sn = snap()
        wsnap("tensor", sn)
        for kc in range(8):
            bi = kc % 2
            bank = pb[bi]
            bank_wait("tensor", bi)

            def mk_tr(e, kc=kc, bank=bank):        sn = snap()
        wsnap("tensor", sn)
        for kc in range(8):
            bi = kc % 2
            bank = pb[bi]
            bank_wait("tensor", bi)

            def mk_tr(e, kc=kc, bank=bank):
                last = None
                for wj in range(4):
                    xn4 = AFV(5120 + wj * 1024, 1024)
                    last = e.transpose(bank[:, wj * 128:(wj + 1) * 128],
                                       xn4[:, kc * 128:(kc + 1) * 128], ident[:, :])
                return last
            inc("tensor", "t", 1, mk_tr)
            tq = cnt["t"]
            hi = ABV(kc * 512, 512)
            lo = ABV(4096 + kc * 512, 512)
            w("scalar", "t", tq)
            inc("scalar", "s", 1, lambda e, hi=hi, bank=bank: e.activation(
                hi, bank[:, 0:512], AF.Copy))
            w("vector", "s")
            inc("vector", "v", 1, lambda e, lo=lo, hi=hi, bank=bank: e.tensor_tensor(
                lo, bank[:, 0:512], hi, ALU.subtract))
            bank_free[bi] = ("v", cnt["v"])
        w("gpsimd", "v")
        dma("gpsimd", "dst", xnth_d[sc], ABV(0, 4096))
        dma("gpsimd", "dst", xntl_d[sc], ABV(4096, 4096))
        w("gpsimd", "dst")
        barrier()

    if debug:
        w("gpsimd", "v")
        w("gpsimd", "s")
        dma("gpsimd", "o", dbg["d_st0"], mcol[:, :])
        dma("gpsimd", "o", dbg["d_st1"], negm[:, :])
        dma("gpsimd", "o", dbg["d_st2"], c1[:, :])
        dma("gpsimd", "o", dbg["d_st3"], c2[:, :])
    bank_wait("tensor", 2)
    inc("tensor", "t", 1, lambda e: e.transpose(pb[2][0:16, 0:128], s_a[:, :], ident[:, :]))
    w("vector", "t")
    inc("vector", "v", 1, lambda e: e.tensor_copy(skT[:, :], pb[2][0:16, 0:128]))
    bank_free[2] = ("v", cnt["v"])
    barrier()

    stream_split_stationary(wk_in, 12288, hi_off=24576, lo_off=32768)
    wk_done = snap()
    AL = AFV(16384, 512, parts=8)
    GL24 = AFV(16896, 512, parts=24)
    ZL = AFV(17408, 512, parts=24)
    RW = AFV(17920, 512, parts=8)
    RAO = AFV(18432, 512, parts=8)
    RK = AFV(18944, 512, parts=8)
    G8 = AFV(19456, 512, parts=8)
    O8 = AFV(19968, 512, parts=8)
    aendT4 = aendT[0:4, 0:8]
    for sc in range(NSC):
        scs = slice(sc * 512, (sc + 1) * 512)
        dma("gpsimd", "dg", ABV(0, 4096), xnth_d[sc])
        dma("gpsimd", "dg", ABV(4096, 4096), xntl_d[sc])
        dma("sync", "din", cossb[:, :], cos_in[:, scs])
        dma("sync", "din", sinsb[:, :], sin_in[:, scs])
        w("tensor", "dg")
        build_sbcast(sc)
        bank_wait("tensor", 7)

        def mk_g(e):
            last = None
            for wj in range(4):
                for kc in range(8):
                    xh = ABV(kc * 512 + wj * 128, 128)
                    xl = ABV(4096 + kc * 512 + wj * 128, 128)
                    wh = wghi[:, kc * 24:(kc + 1) * 24]
                    wl = wglo[:, kc * 24:(kc + 1) * 24]
                    reg = pb[7][0:24, wj * 128:wj * 128 + 128]
                    e.matmul(reg, wh, xh, start=(kc == 0), stop=False)
                    e.matmul(reg, wh, xl, start=False, stop=False)
                    last = e.matmul(reg, wl, xh, start=False, stop=(kc == 7))
            return last
        inc("tensor", "t", 1, mk_g)
        w("vector", "t")
        w("vector", "s")

        def mk_zs(e, ZL=ZL):
            last = None
            for wj in range(4):
                last = e.tensor_tensor(ZL[0:24, wj * 128:(wj + 1) * 128],
                                       pb[7][0:24, wj * 128:wj * 128 + 128],
                                       sbcast[0:24, wj * 128:(wj + 1) * 128], ALU.mult)
            return last
        inc("vector", "v", 1, mk_zs)
        bank_free[7] = ("v", cnt["v"])
        w("scalar", "v")
        inc("scalar", "s", 1, lambda e, GL24=GL24, ZL=ZL: e.activation(
            GL24, ZL, AF.Sigmoid, bias=bgcol[:, 0:1]))
        w("gpsimd", "s")
        dma("gpsimd", "dg", G8, GL24[8:16, 0:512])
        dma("gpsimd", "dg", O8, GL24[16:24, 0:512])
        w("vector", "s")
        for wj in range(4):
            lw = wj * 128
            inc("vector", "v", 1, lambda e, lw=lw, GL24=GL24, AL=AL: e.tensor_tensor_scan(
                AL[0:8, lw:lw + 128], GL24[0:8, lw:lw + 128],
                zeros8[0:8, 0:128], 1.0, ALU.mult, ALU.add))
        w("vector", "v")
        for wj in range(4):
            tc = sc * 4 + wj
            lw = wj * 128
            run("vector", lambda e, tc=tc, lw=lw, AL=AL: e.tensor_copy(
                aends[0:8, tc:tc + 1], AL[0:8, lw + 127:lw + 128]))
        inc("vector", "v", 1, lambda e, RW=RW, AL=AL: e.reciprocal(RW[0:8, :], AL[0:8, :]))
        w("vector", "v")
        inc("vector", "v", 1, lambda e, RW=RW: e.tensor_scalar(
            RW[0:8, :], RW[0:8, :], SCALE, None, ALU.mult))
        w("vector", "v")
        w("vector", "dg")
        inc("vector", "v", 1, lambda e, RW=RW, G8=G8: e.tensor_tensor(
            RW[0:8, :], RW[0:8, :], G8[0:8, :], ALU.mult))
        w("vector", "v")
        run("vector", lambda e, RAO=RAO, O8=O8, AL=AL: e.tensor_tensor(
            RAO[0:8, :], AL[0:8, :], O8[0:8, :], ALU.mult))
        for wj in range(4):
            tc = sc * 4 + wj
            lw = wj * 128
            run("vector", lambda e, lw=lw, tc=tc, RW=RW, RK=RK: e.tensor_scalar(
                RK[0:8, lw:lw + 128], RW[0:8, lw:lw + 128],
                aends[0:8, tc:tc + 1], None, ALU.mult))
        inc("vector", "v", 1, lambda e: e.engine_nop())
        sn = snap()
        wsnap("tensor", sn)
        bank_wait("tensor", 6)

        def mk_cols(e, sc=sc, RW=RW, RAO=RAO, RK=RK):
            last = None
            for wj in range(4):
                lw = wj * 128
                for bd, RB in ((0, RW), (8, RAO), (16, RK)):
                    last = e.transpose(pb[6][:, wj * 32 + bd:wj * 32 + bd + 8],
                                       RB[0:8, lw:lw + 128], ident[0:8, 0:8])
            return last
        inc("tensor", "t", 1, mk_cols)
        w("scalar", "t")

        def mk_cols_ev(e, sc=sc):
            last = None
            for wj in range(4):
                tc = sc * 4 + wj
                last = e.activation(cols[:, tc * 24:(tc + 1) * 24],
                                    pb[6][:, wj * 32:wj * 32 + 24], AF.Copy)
            return last
        inc("scalar", "s", 1, mk_cols_ev)
        bank_free[6] = ("s", cnt["s"])
        bank_wait("tensor", 2)
        inc("tensor", "t", 1, lambda e, sc=sc: e.transpose(
            pb[2][0:4, 0:8], aends[0:8, sc * 4:(sc + 1) * 4], ident[0:8, 0:8]))
        w("vector", "t")
        inc("vector", "v", 1, lambda e: e.tensor_copy(aendT4, pb[2][0:4, 0:8]))
        bank_free[2] = ("v", cnt["v"])
        w("tensor", "v")
        bank_wait("tensor", 2)

        def mk_ab(e, sc=sc):
            last = None
            for wj in range(4):
                last = e.matmul(pb[2][:, wj * 8:(wj + 1) * 8],
                                esel[0:4, wj * 128:(wj + 1) * 128],
                                aendT4, start=True, stop=True)
            return last
        inc("tensor", "t", 1, mk_ab)
        w("scalar", "t")
        inc("scalar", "s", 1, lambda e, sc=sc: e.activation(
            abcol[:, sc * 32:(sc + 1) * 32], pb[2][:, 0:32], AF.Copy))
        bank_free[2] = ("s", cnt["s"])
        wsnap("tensor", wv_done)
        for fh in range(2):
            for wj in range(4):
                tc = sc * 4 + wj
                bi = (fh * 4 + wj) % 2
                bank = pb[bi]
                bank_wait("tensor", bi)

                def mk_v(e, fh=fh, wj=wj, bank=bank):
                    last = None
                    for kc in range(8):
                        xh = ABV(kc * 512 + wj * 128, 128)
                        xl = ABV(4096 + kc * 512 + wj * 128, 128)
                        wh = ABV(8192 + kc * 1024 + fh * 512, 512)
                        wl = ABV(16384 + kc * 1024 + fh * 512, 512)
                        e.matmul(bank[:, 0:512], xh, wh, start=(kc == 0), stop=False)
                        e.matmul(bank[:, 0:512], xh, wl, start=False, stop=False)
                        last = e.matmul(bank[:, 0:512], xl, wh, start=False, stop=(kc == 7))
                    return last
                inc("tensor", "t", 1, mk_v)
                w("vector", "t")
                dst = AFV(wj * 1024 + fh * 512, 512)
                inc("vector", "v", 1, lambda e, dst=dst, bank=bank, tc=tc: e.tensor_scalar(
                    dst, bank[:, 0:512], s_a[:, tc:tc + 1], None, ALU.mult))
                bank_free[bi] = ("v", cnt["v"])
        wsnap("tensor", wk_done)
        proj_stationary_weight(4096, wbase=(24576, 32768))
        sn = snap()
        wsnap("vector", sn)
        wsnap("gpsimd", sn)
        rope(4096, 8192, 10240)
        rope_done = snap()
        for wj in range(4):
            tc = sc * 4 + wj
            for grp in range(2):
                bi = 2 + grp
                bank = pb[bi]
                bank_wait("tensor", bi)
                wsnap("tensor", rope_done)

                def mk_ktr(e, wj=wj, grp=grp, bank=bank):
                    last = None
                    for fq in range(4):
                        fc = grp * 4 + fq
                        src = AFV(4096 + fc * 512 + wj * 128, 128)
                        last = e.transpose(bank[:, fq * 128:(fq + 1) * 128],
                                           src, ident[:, :])
                    return last
                inc("tensor", "t", 1, mk_ktr)
                tq = cnt["t"]
                w("vector", "t", tq)
                for fq in range(4):
                    for sub in range(2):
                        h = 2 * fq + sub
                        d_off = 15360 + h * 128 + grp * 64
                        s_c = fq * 128 + sub * 64
                        ccol = cols[:, tc * 24 + 16 + h:tc * 24 + 17 + h]
                        inc("vector", "v", 1, lambda e, d=d_off, s2=s_c, bank=bank, ccol=ccol: e.tensor_scalar(
                            AFV(d, 64), bank[:, s2:s2 + 64], ccol[:, 0:1], None, ALU.mult))
                bank_free[bi] = ("v", cnt["v"])
            sn2 = snap()
            wsnap("tensor", sn2)
            for half in range(2):
                bi = 4 + half
                bank = pb[bi]
                bank_wait("tensor", bi)

                def mk_u(e, wj=wj, half=half, bank=bank):
                    last = None
                    for h in range(8):
                        b_h = (h % 2) * 64
                        kp = AFV(15360 + h * 128 + half * 64, 64)
                        vv = AFV(wj * 1024 + h * 128, 128)
                        reg = bank[b_h:b_h + 64,
                                   (h // 2) * 128:(h // 2) * 128 + 128]
                        last = e.matmul(reg, kp, vv, start=True, stop=True,
                                        tile_position=(0, b_h))
                    return last
                inc("tensor", "t", 1, mk_u)
                w("scalar", "t")
                inc("scalar", "s", 1, lambda e, half=half, bank=bank: e.activation(
                    AFV(14336 + half * 512, 512), bank[:, 0:512], AF.Copy))
                bank_free[bi] = ("s", cnt["s"])
            w("vector", "s")
            for h in range(8):
                b_h = (h % 2) * 64
                co = (h // 2) * 128

                def mviews2(h=h, b_h=b_h, co=co):
                    mv = Mloc.ap()[b_h:b_h + 64, :].rearrange(
                        "p (two rest) -> p two rest", two=2)[:, :, co:co + 128]
                    uv = AFV(14336, 1024, parts=64, p0=b_h).rearrange(
                        "p (two rest) -> p two rest", two=2)[:, :, co:co + 128]
                    return mv, uv
                run("vector", lambda e, h=h, tc=tc, b_h=b_h, mviews2=mviews2: (
                    lambda mv_uv: e.scalar_tensor_tensor(
                        mv_uv[0], mv_uv[0],
                        abcol[b_h:b_h + 64, tc * 8 + h:tc * 8 + h + 1],
                        mv_uv[1], ALU.mult, ALU.add))(mviews2()))
            inc("vector", "v", 1, lambda e: e.engine_nop())
            w("gpsimd", "v")
            dma("gpsimd", "dst", u_d[tc], AFV(14336, 1024))
            w("scalar", "dst")
        sn3 = snap()
        wsnap("gpsimd", sn3)
        dma("gpsimd", "dst", kt_d[sc], AFV(4096, 4096))
        dma("gpsimd", "dst", v_d[sc], AFV(0, 4096))
        if debug and sc == 3:
            dma("gpsimd", "o", dbg["d_xnh"], ABV(0, 512))
            dma("gpsimd", "o", dbg["d_xnl"], ABV(4096, 512))
            dma("gpsimd", "o", dbg["d_gl"], GL24)
            dma("gpsimd", "o", dbg["d_zl"], ZL)
            dma("gpsimd", "o", dbg["d_al"], AL)
            dma("gpsimd", "o", dbg["d_sb"], sbcast[:, :])
        w("gpsimd", "dst")
        barrier()

    dma("gpsimd", "dst", mex_i, Mloc[:, :])
    inc("vector", "v", 1, lambda e: e.tensor_tensor_scan(
        ascan[:, :], aends[:, :], zeros8[0:8, 0:16], 1.0, ALU.mult, ALU.add))
    w("tensor", "v")
    bank_wait("tensor", 2)
    inc("tensor", "t", 1, lambda e: e.transpose(
        pb[2][0:1, 0:8], ascan[0:8, 15:16], ident[0:8, 0:8]))
    w("scalar", "t")
    inc("scalar", "s", 1, lambda e: e.activation(asegrow[:, :], pb[2][0:1, 0:8], AF.Copy))
    bank_free[2] = ("s", cnt["s"])
    w("gpsimd", "s")
    dma("gpsimd", "dst", aseg_i, asegrow[:, :])
    w("gpsimd", "dst")
    inc("gpsimd", "cc", 1, lambda e: e.collective_compute(
        "AllGather", ALU.bypass, replica_groups=[list(range(NCORES))],
        ins=[mex_i.opt()], outs=[mex_o.opt()]))
    inc("gpsimd", "cc", 1, lambda e: e.collective_compute(
        "AllGather", ALU.bypass, replica_groups=[list(range(NCORES))],
        ins=[aseg_i.opt()], outs=[aseg_o.opt()]))
    stream_split_stationary(wq_in, 16384)
    px_wq_done = snap()

    wq_done = snap()
    yev_state = {}

    def emit_y(h, wj, tc, pev, mups):
        ybi = 3 if h < 4 else 4
        ybank = pb[ybi]
        if h % 4 == 0:
            bank_wait("tensor", ybi)
        w("tensor", "v", pev[h])

        def mk_y(e, h=h, wj=wj, ybank=ybank):
            fc1 = h // 2
            fc2 = 4 + h // 2
            p0 = (h % 2) * 64
            qx1 = AFV(fc1 * 512 + wj * 128, 128, parts=64, p0=p0)
            qx2 = AFV(fc2 * 512 + wj * 128, 128, parts=64, p0=p0)
            vv = AFV(8192 + wj * 1024 + h * 128, 128)
            b_h = (h % 2) * 64
            co = (h // 2) * 128
            reg = ybank[:, (h % 4) * 128:(h % 4 + 1) * 128]
            e.matmul(reg, Pbuf[h % 2][:, :], vv, start=True, stop=False)
            e.matmul(reg, qx1, M_sb[b_h:b_h + 64, co:co + 128],
                     start=False, stop=False)
            return e.matmul(reg, qx2, M_sb[b_h:b_h + 64, 512 + co:512 + co + 128],
                            start=False, stop=True)
        inc("tensor", "t", 1, mk_y)
        if h % 4 == 3:
            tq = cnt["t"]
            w("vector", "t", tq)

            def mk_yev(e, wj=wj, h=h, ybank=ybank, tc=tc):
                last = None
                for hq in range(4):
                    hh = (h // 4) * 4 + hq
                    last = e.tensor_scalar(
                        AFV(12288 + wj * 1024 + hh * 128, 128),
                        ybank[:, hq * 128:(hq + 1) * 128],
                        cols[:, tc * 24 + 8 + hh:tc * 24 + 9 + hh],
                        None, ALU.mult)
                return last
            inc("vector", "v", 1, mk_yev)
            bank_free[ybi] = ("v", cnt["v"])

    Pbuf = [P_sb, P_sb2]
    pb_xslot = [(0, 4096), (32768, 36864)]
    pb_xn_dg = {}

    def pb_xn_prefetch(sc):
        par = sc % 2
        dma("gpsimd", "dg", ABV(pb_xslot[par][0], 4096), xnth_d[sc])
        dma("gpsimd", "dg", ABV(pb_xslot[par][1], 4096), xntl_d[sc])
        pb_xn_dg[sc] = ("dg", dtok("dg"))

    pb_rope_done = {"v": 0, "g": 0}
    pb_kv_dg = {}

    def pb_head(sc, hoisted=False):
        par = sc % 2
        scs = slice(sc * 512, (sc + 1) * 512)
        if hoisted:
            dma("sync", "din", ABV(pb_xslot[par][0], 4096), xnth_d[sc])
            dma("sync", "din", ABV(pb_xslot[par][1], 4096), xntl_d[sc])
            pb_xn_dg[sc] = ("din", dtok("din"))
            dma("sync", "din", AFV(4096, 4096), kt_d[sc])
            dma("sync", "din", AFV(8192, 4096), v_d[sc])
            pb_kv_dg[sc] = ("din", dtok("din"))
        else:
            w("gpsimd", "t")
            dma("gpsimd", "dg", AFV(4096, 4096), kt_d[sc])
            dma("gpsimd", "dg", AFV(8192, 4096), v_d[sc])
            pb_kv_dg[sc] = ("dg", dtok("dg"))
            w("sync", "v", pb_rope_done["v"])
            w("sync", "g", pb_rope_done["g"])
        dma("sync", "din", cossb[:, :], cos_in[:, scs])
        dma("sync", "din", sinsb[:, :], sin_in[:, scs])
        sname, sval = pb_xn_dg[sc]
        w("tensor", sname, sval)
        wsnap("tensor", wq_done)
        build_sbcast(sc)
        proj_stationary_weight(0, xbase=pb_xslot[par])
        if sc + 1 < NSC:
            w("gpsimd", "t")
            pb_xn_prefetch(sc + 1)
        sn = snap()
        wsnap("vector", sn)
        wsnap("gpsimd", sn)
        if sc > 0:
            w("vector", "t")
            w("gpsimd", "t")
        rope(0, 12288, 14336, eng2=("vector" if hoisted else "gpsimd"))
        rope_done = snap()
        pb_rope_done["v"] = cnt["v"]
        pb_rope_done["g"] = cnt["g"]
        wsnap("tensor", rope_done)
        sname, sval = pb_kv_dg[sc]
        w("tensor", sname, sval)

    pb_head(0, hoisted=True)
    w("gpsimd", "cc")
    dma("gpsimd", "dg", asegall[:, :], aseg_o)
    w("vector", "dg")
    inc("vector", "v", 1, lambda e: e.scalar_tensor_tensor(
        alpharows[0:8, :], asegall[0:8, :], selsb[0:8, 0:1],
        onemsb[0:8, :], ALU.mult, ALU.add))
    w("tensor", "v")
    bank_wait("tensor", 2)

    def mk_albc(e):
        for j in range(7):
            e.matmul(pb[2][:, j * 8:(j + 1) * 8],
                     esel[0:8, j * 128:(j + 1) * 128],
                     alpharows[0:8, 0:8], start=True, stop=True)
        return e.matmul(pb[2][:, 56:64], ones1[:, :], selrowsb[0:1, 0:8],
                        start=True, stop=True)
    inc("tensor", "t", 1, mk_albc)
    w("scalar", "t")
    run("scalar", lambda e: e.activation(albc[:, 0:56], pb[2][:, 0:56], AF.Copy))
    inc("scalar", "s", 1, lambda e: e.activation(selbc[:, 0:8], pb[2][:, 56:64], AF.Copy))
    bank_free[2] = ("s", cnt["s"])
    w("vector", "s")
    Mtmp = AFV(16384, 1024)
    Mterm2 = AFV(17408, 1024)
    for j in range(7):
        w("gpsimd", "v")
        dma("gpsimd", "dg", Mtmp, mex_o[j * 128:(j + 1) * 128, :])
        w("vector", "dg")
        run("vector", lambda e, j=j, Mtmp=Mtmp: e.tensor_scalar(
            Mterm2[:, :], Mtmp[:, :], selbc[:, j:j + 1], None, ALU.mult))
        w("vector", "v")
        for h in range(8):
            b_h = (h % 2) * 64
            for half in range(2):
                co = half * 512 + (h // 2) * 128
                run("vector", lambda e, h=h, j=j, b_h=b_h, co=co: e.scalar_tensor_tensor(
                    M_sb[b_h:b_h + 64, co:co + 128], M_sb[b_h:b_h + 64, co:co + 128],
                    albc[b_h:b_h + 64, j * 8 + h:j * 8 + h + 1],
                    Mterm2[b_h:b_h + 64, co:co + 128], ALU.mult, ALU.add))
        inc("vector", "v", 1, lambda e: e.engine_nop())
    if debug:
        w("gpsimd", "v")
        dma("gpsimd", "o", dbg["d_min"], M_sb[:, :])
        dma("gpsimd", "o", dbg["d_mloc"], Mloc[:, :])
        dma("gpsimd", "o", dbg["d_cols"], cols[:, :])
        dma("gpsimd", "o", dbg["d_abcol"], abcol[:, :])
        dma("gpsimd", "o", dbg["d_aends"], aends[:, :])
        dma("gpsimd", "o", dbg["d_stats"][:, 0:16], rstd_a[:, :])
        dma("gpsimd", "o", dbg["d_stats"][:, 16:32], nmr_a[:, :])
        dma("gpsimd", "o", dbg["d_stats"][:, 32:48], s_a[:, :])
    barrier()


    for sc in range(NSC):
        par = sc % 2
        if sc > 0:
            pb_head(sc)
        for wj in range(4):
            tc = sc * 4 + wj
            w("gpsimd", "v")
            dma("gpsimd", "dg", AFV(16384, 1024), u_d[tc])
            udg = dtok("dg")
            mups = 0
            pev = {}
            for h in range(8):
                sbi = 2 if (h % 2 == 0) else 5
                sbank = pb[sbi]
                bank_wait("tensor", sbi)

                def mk_s(e, h=h, wj=wj, sbank=sbank):
                    fc1 = h // 2
                    fc2 = 4 + h // 2
                    p0 = (h % 2) * 64
                    kx1 = AFV(4096 + fc1 * 512 + wj * 128, 128, parts=64, p0=p0)
                    kx2 = AFV(4096 + fc2 * 512 + wj * 128, 128, parts=64, p0=p0)
                    qx1 = AFV(fc1 * 512 + wj * 128, 128, parts=64, p0=p0)
                    qx2 = AFV(fc2 * 512 + wj * 128, 128, parts=64, p0=p0)
                    e.matmul(sbank[:, 0:128], kx1, qx1, start=True, stop=False)
                    return e.matmul(sbank[:, 0:128], kx2, qx2, start=False, stop=True)
                inc("tensor", "t", 1, mk_s)
                w("vector", "t")
                inc("vector", "v", 1, lambda e, sbank=sbank, tc=tc, h=h: e.scalar_tensor_tensor(
                    Pbuf[h % 2][:, :], sbank[:, 0:128],
                    cols[:, tc * 24 + h:tc * 24 + h + 1], tri[:, :],
                    ALU.mult, ALU.mult))
                pev[h] = cnt["v"]
                bank_free[sbi] = ("v", cnt["v"])
                if h >= 1:
                    emit_y(h - 1, wj, tc, pev, mups)
            emit_y(7, wj, tc, pev, mups)
            w("vector", "dg", udg)
            w("vector", "t")
            for h in range(8):
                b_h = (h % 2) * 64
                co = (h // 2) * 128

                def mviews(h=h, b_h=b_h, co=co):
                    mv = M_sb.ap()[b_h:b_h + 64, :].rearrange(
                        "p (two rest) -> p two rest", two=2)[:, :, co:co + 128]
                    uv = AFV(16384, 1024, parts=64, p0=b_h).rearrange(
                        "p (two rest) -> p two rest", two=2)[:, :, co:co + 128]
                    return mv, uv
                fn = lambda e, h=h, tc=tc, b_h=b_h, mviews=mviews: (
                    lambda mv_uv: e.scalar_tensor_tensor(
                        mv_uv[0], mv_uv[0],
                        abcol[b_h:b_h + 64, tc * 8 + h:tc * 8 + h + 1],
                        mv_uv[1], ALU.mult, ALU.add))(mviews())
                if h == 7:
                    inc("vector", "v", 1, fn)
                else:
                    run("vector", fn)
        if debug and sc == 0:
            w("gpsimd", "v")
            dma("gpsimd", "o", dbg["d_y"], AFV(12288, 4096))
        sn = snap()
        wsnap("tensor", sn)
        for kc in range(8):
            bi = 6 + (kc % 2)
            bank = pb[bi]
            bank_wait("tensor", bi)

            def mk_ytr(e, kc=kc, bank=bank):
                last = None
                for wj in range(4):
                    src = AFV(12288 + wj * 1024 + kc * 128, 128)
                    last = e.transpose(bank[:, wj * 128:(wj + 1) * 128], src, ident[:, :])
                return last
            inc("tensor", "t", 1, mk_ytr)
            w("scalar", "t")
            if sc > 0 and kc == 0:
                w("scalar", "dst", pb_spill[0])
                w("vector", "dst", pb_spill[0])
            hi = ABV(24576 + kc * 512, 512)
            lo = ABV(28672 + kc * 512, 512)
            inc("scalar", "s", 1, lambda e, hi=hi, bank=bank: e.activation(
                hi, bank[:, 0:512], AF.Copy))
            w("vector", "s")
            inc("vector", "v", 1, lambda e, lo=lo, hi=hi, bank=bank: e.tensor_tensor(
                lo, bank[:, 0:512], hi, ALU.subtract))
            bank_free[bi] = ("v", cnt["v"])
        w("gpsimd", "v")
        dma("gpsimd", "dst", yth_d[sc], ABV(24576, 4096))
        dma("gpsimd", "dst", ytl_d[sc], ABV(28672, 4096))
        pb_spill = (dtok("dst"),)
    w("gpsimd", "dst")
    barrier()

    stream_split_moving(wo_in, False, 16384)
    wo_done = snap()
    wo_stage_free = dict(s=cnt["s"], v=cnt["v"])
    dma("sync", "din", AFV(20480, 1024), boutbc_in)
    boutd = dtok("din")
    w("vector", "din", boutd)

    presplit_pieces = [(e_i, pc2) for e_i in range(8) for pc2 in range(4)]
    pre_state = {"i": 0, "dst_done": 0}

    def emit_presplit(n):
        for _ in range(n):
            i = pre_state["i"]
            if i >= len(presplit_pieces):
                return
            pre_state["i"] += 1
            e_i, pc2 = presplit_pieces[i]
            stage = AFV(16384, 2048)
            lo_bf = AFV(19456, 1024).bitcast(BF16)
            if i == 0:
                w("sync", "v", wo_stage_free["v"])
            else:
                w("sync", "dst")
            dma("sync", "din", stage, we_in[e_i][:, pc2 * 2048:(pc2 + 1) * 2048])
            hi_bf = AFV(18432, 1024).bitcast(BF16)
            w("scalar", "din")
            inc("scalar", "s", 1, lambda e, hi=hi_bf, st=stage: e.activation(hi, st, AF.Copy))
            w("vector", "s")
            inc("vector", "v", 1, lambda e, lo=lo_bf, st=stage, hi=hi_bf: e.tensor_tensor(
                lo, st, hi, ALU.subtract))
            w("gpsimd", "v")
            dma("gpsimd", "dst", weh_d[e_i][:, pc2 * 2048:(pc2 + 1) * 2048], hi_bf)
            dma("gpsimd", "dst", wel_d[e_i][:, pc2 * 2048:(pc2 + 1) * 2048], lo_bf)

    pc_yslot = [(24576, 28672), (32768, 36864)]
    pc_dg = {}
    for s2 in (0, 1):
        par = s2 % 2
        dma("gpsimd", "dg", ABV(pc_yslot[par][0], 4096), yth_d[s2])
        dma("gpsimd", "dg", ABV(pc_yslot[par][1], 4096), ytl_d[s2])
        pc_dg[s2] = dtok("dg")
    for sc in range(NSC):
        par = sc % 2
        ysl = pc_yslot[par]
        w("tensor", "dg", pc_dg[sc])
        wsnap("tensor", wo_done)
        for fh in range(2):
            for wj in range(4):
                tc = sc * 4 + wj
                bi = (fh * 4 + wj) % 2
                bank = pb[bi]
                bank_wait("tensor", bi)

                def mk_o(e, fh=fh, wj=wj, bank=bank, ysl=ysl):
                    last = None
                    for kc in range(8):
                        yh = ABV(ysl[0] + kc * 512 + wj * 128, 128)
                        yl = ABV(ysl[1] + kc * 512 + wj * 128, 128)
                        wh = ABV(8192 + kc * 1024 + fh * 512, 512)
                        wl = ABV(16384 + kc * 1024 + fh * 512, 512)
                        e.matmul(bank[:, 0:512], yh, wh, start=(kc == 0), stop=False)
                        e.matmul(bank[:, 0:512], yh, wl, start=False, stop=False)
                        last = e.matmul(bank[:, 0:512], yl, wh, start=False, stop=(kc == 7))
                    return last
                inc("tensor", "t", 1, mk_o)
                w("vector", "t")
                dst = AFV(tc * 1024 + fh * 512, 512)
                inc("vector", "v", 1, lambda e, dst=dst, bank=bank, fh=fh: e.scalar_tensor_tensor(
                    dst, bank[:, 0:512], 1.0, AFV(20480 + fh * 512, 512),
                    ALU.mult, ALU.add))
                bank_free[bi] = ("v", cnt["v"])
                emit_presplit(1)
        if sc + 2 < NSC:
            w("gpsimd", "t")
            dma("gpsimd", "dg", ABV(pc_yslot[par][0], 4096), yth_d[sc + 2])
            dma("gpsimd", "dg", ABV(pc_yslot[par][1], 4096), ytl_d[sc + 2])
            pc_dg[sc + 2] = dtok("dg")
    emit_presplit(99)
    w("gpsimd", "dst")
    barrier()

    for sc in range(NSC):
        if sc > 0:
            w("gpsimd", "t")
        dma("gpsimd", "dg", ABV(0, 4096), xnth_d[sc])
        dma("gpsimd", "dg", ABV(4096, 4096), xntl_d[sc])
        w("tensor", "dg")
        bank_wait("tensor", 2)

        def mk_sc(e, sc=sc):
            last = None
            for wj in range(4):
                for kc in range(8):
                    xh = ABV(kc * 512 + wj * 128, 128)
                    xl = ABV(4096 + kc * 512 + wj * 128, 128)
                    gh = gwhi[:, kc * 8:(kc + 1) * 8]
                    gl = gwlo[:, kc * 8:(kc + 1) * 8]
                    reg = pb[2][:, wj * 8:(wj + 1) * 8]
                    e.matmul(reg, xh, gh, start=(kc == 0), stop=False)
                    e.matmul(reg, xh, gl, start=False, stop=False)
                    last = e.matmul(reg, xl, gh, start=False, stop=(kc == 7))
            return last
        inc("tensor", "t", 1, mk_sc)
        w("scalar", "t")
        w("scalar", "v")
        inc("scalar", "s", 1, lambda e: e.activation(scores8[:, 0:32], pb[2][:, 0:32], AF.Copy))
        bank_free[2] = ("s", cnt["s"])
        w("vector", "s")
        for wj in range(4):
            tc = sc * 4 + wj
            s8 = scores8[:, wj * 8:(wj + 1) * 8]
            inc("vector", "v", 1, lambda e, s8=s8: e.reduce_max(mx1[:, :], s8, axis=AX.X))
            w("vector", "v")
            inc("vector", "v", 1, lambda e, s8=s8: e.tensor_scalar(
                sc8a[:, :], s8, mx1[:, 0:1], None, ALU.is_ge))
            w("vector", "v")
            inc("vector", "v", 1, lambda e: e.tensor_tensor_scan(
                sc8b[:, :], sc8a[:, :], zero128[:, :], 0.0, ALU.add, ALU.add))
            w("vector", "v")
            inc("vector", "v", 1, lambda e: e.tensor_scalar(
                sc8c[:, :], sc8b[:, :], 1.0, None, ALU.is_equal))
            w("vector", "v")
            inc("vector", "v", 1, lambda e, tc=tc: e.tensor_tensor(
                onehot[:, tc * 8:(tc + 1) * 8], sc8a[:, :], sc8c[:, :], ALU.mult))
    if debug:
        w("gpsimd", "v")
        dma("gpsimd", "o", dbg["d_onehot"], onehot[:, :])
    barrier()
    xslot = [(0, 4096), (24576, 28672)]
    xnT_ready = {}
    hi_slot = [lambda off, wdt: ABV(8192 + off, wdt),
               lambda off, wdt: AFV(16384 + off // 2, (wdt + 1) // 2).bitcast(BF16)]
    wdg_hi = {}
    w("gpsimd", "t")
    dma("gpsimd", "dg", ABV(8192, 8192), weh_d[0])
    wdg_hi[0] = dtok("dg")
    for e_i in range(8):
        w("gpsimd", "t")
        dma("gpsimd", "dg", ABV(16384, 8192), wel_d[e_i])
        wdg = dtok("dg")
        for sc in range(NSC):
            par = sc % 2
            if e_i == 0 and sc < 2:
                dma("gpsimd", "dg", ABV(xslot[par][0], 4096), xnth_d[sc])
                dma("gpsimd", "dg", ABV(xslot[par][1], 4096), xntl_d[sc])
                xnT_ready[(e_i, sc)] = dtok("dg")
            w("tensor", "dg", xnT_ready[(e_i, sc)])
            w("tensor", "dg", wdg)
            for fh in range(2):
                for wj in range(4):
                    tc = sc * 4 + wj
                    bi = (fh * 4 + wj) % 2
                    bank = pb[bi]
                    bank_wait("tensor", bi)

                    def mk_e(e, fh=fh, wj=wj, bank=bank, par=par, e_i=e_i):
                        last = None
                        for kc in range(8):
                            xh = ABV(xslot[par][0] + kc * 512 + wj * 128, 128)
                            xl = ABV(xslot[par][1] + kc * 512 + wj * 128, 128)
                            wh = hi_slot[e_i % 2](kc * 1024 + fh * 512, 512)
                            wl = ABV(16384 + kc * 1024 + fh * 512, 512)
                            e.matmul(bank[:, 0:512], xh, wh, start=(kc == 0), stop=False)
                            e.matmul(bank[:, 0:512], xh, wl, start=False, stop=False)
                            last = e.matmul(bank[:, 0:512], xl, wh, start=False, stop=(kc == 7))
                        return last
                    inc("tensor", "t", 1, mk_e)
                    w("vector", "t")
                    dst = AFV(tc * 1024 + fh * 512, 512)
                    inc("vector", "v", 1, lambda e, dst=dst, bank=bank, tc=tc, e_i=e_i: e.scalar_tensor_tensor(
                        dst, bank[:, 0:512], onehot[:, tc * 8 + e_i:tc * 8 + e_i + 1],
                        dst, ALU.mult, ALU.add))
                    bank_free[bi] = ("v", cnt["v"])
            tdone = cnt["t"]
            nxt = (e_i, sc + 2) if sc + 2 < NSC else (e_i + 1, (sc + 2) % NSC)
            if nxt[0] < 8:
                npar = nxt[1] % 2
                w("gpsimd", "t", tdone)
                dma("gpsimd", "dg", ABV(xslot[npar][0], 4096), xnth_d[nxt[1]])
                dma("gpsimd", "dg", ABV(xslot[npar][1], 4096), xntl_d[nxt[1]])
                xnT_ready[nxt] = dtok("dg")
            if sc == 2 and e_i + 1 < 8:
                hs = hi_slot[(e_i + 1) % 2](0, 8192)
                dma("gpsimd", "dg", hs, weh_d[e_i + 1])
                wdg_hi[e_i + 1] = dtok("dg")
    barrier()

    if debug:
        dma("gpsimd", "o", dbg["d_ASC"], AFV(0, 4096))
    pe_vdone = {}
    for tc in range(NCH):
        xst = AFV(16384 + (tc % 2) * 1024, 1024)
        if tc >= 2:
            w("sync", "v", pe_vdone[tc - 2])
        dma("sync", "din", xst, x_in[tc * 128:(tc + 1) * 128, :])
        w("vector", "din")
        xn8 = AFV(18432, 1024)
        run("vector", lambda e, xst=xst, tc=tc, xn8=xn8: e.tensor_scalar(
            xn8, xst, rstd_a[:, tc:tc + 1], nmr_a[:, tc:tc + 1], ALU.mult, ALU.add))
        dst = AFV(tc * 1024, 1024)
        run("vector", lambda e, dst=dst, xst=xst: e.tensor_tensor(dst, dst, xst, ALU.add))
        w("vector", "v")
        inc("vector", "v", 1, lambda e, dst=dst, xn8=xn8: e.tensor_tensor(dst, dst, xn8, ALU.add))
        pe_vdone[tc] = cnt["v"]
        w("sync", "v", pe_vdone[tc])
        dma("sync", "o", out_ext[tc * 128:(tc + 1) * 128, :], dst)
    w("sync", "o")

    with nc.Block() as block:
        for eng in ENGS:
            def runner(e, lst=plan[eng]):
                for fn in lst:
                    fn(e)
            getattr(block, eng)(runner)
    stack.close()
    return nc


_PERM = np.concatenate([
    np.concatenate([np.arange(h * 128, h * 128 + 64) for h in range(NH)]),
    np.concatenate([np.arange(h * 128 + 64, h * 128 + 128) for h in range(NH)]),
])


def _host_inputs(inputs):
    f32 = np.float32
    x = np.ascontiguousarray(np.asarray(inputs["L0_attn_x"], f32).reshape(L, HID))
    norm_w = np.asarray(inputs["L0_attn_norm_w"], f32)
    Wq = np.asarray(inputs["L0_attn_Wq"], f32)
    Wk = np.asarray(inputs["L0_attn_Wk"], f32)
    Wv = np.asarray(inputs["L0_attn_Wv"], f32)
    Wout = np.asarray(inputs["L0_attn_Wout"], f32)
    bout = np.asarray(inputs["L0_attn_bout"], f32)
    Wbeta = np.asarray(inputs["L0_attn_Wbeta"], f32)
    Wig = np.asarray(inputs["L0_attn_Wig"], f32)
    Wog = np.asarray(inputs["L0_attn_Wog"], f32)
    bg = np.concatenate([np.asarray(inputs["L0_attn_bbeta"], f32),
                         np.asarray(inputs["L0_attn_big"], f32),
                         np.asarray(inputs["L0_attn_bog"], f32)])
    gate_w = np.asarray(inputs["L0_ffn_gate_w"], f32)
    expert_w = np.asarray(inputs["L0_ffn_expert_w"], f32)

    def stat_layout(wt):
        return np.ascontiguousarray(
            wt.reshape(8, 128, 8, 128).transpose(2, 1, 0, 3).reshape(8, 128, 1024))

    def mov_layout(wt):
        return np.ascontiguousarray(
            wt.reshape(8, 128, wt.shape[1]).transpose(1, 0, 2).reshape(128, 8 * wt.shape[1]))

    wq_h = stat_layout(Wq.T[:, _PERM])
    wk_h = stat_layout(Wk.T[:, _PERM])
    wv_h = mov_layout(Wv.T)
    wo_h = mov_layout(Wout.T)
    we_h = np.ascontiguousarray(np.stack([mov_layout(expert_w[e].T) for e in range(8)]))
    wg_h = mov_layout(np.concatenate([Wbeta, Wig, Wog], axis=0).T)
    gw_h = mov_layout(gate_w.T)
    nwc_h = np.ascontiguousarray(norm_w.reshape(8, 128).T)
    boutbc = np.ascontiguousarray(np.broadcast_to(bout[None, :], (128, HID)))
    tri_h = np.triu(np.ones((128, 128), f32))
    ident_h = np.eye(128, dtype=f32)
    ones1_h = np.ones((1, 128), f32)

    half = 64
    inv_freq = (1.0 / 10000.0 ** (np.arange(half, dtype=f32) / half)).astype(f32)
    pos = np.arange(L, dtype=f32)
    freqs = pos[:, None] * inv_freq[None, :]
    cos_t = np.cos(freqs).astype(f32)
    sin_t = np.sin(freqs).astype(f32)

    in_maps = []
    for c in range(NCORES):
        sel = np.zeros((8, 1), f32)
        sel[:c, 0] = 1.0
        onems = np.ascontiguousarray(np.broadcast_to(1.0 - sel, (8, 8)))
        selrow = np.zeros((1, 8), f32)
        selrow[0, :7] = sel[:7, 0]
        cslice = slice(c * T, (c + 1) * T)
        cosr = np.ascontiguousarray(np.tile(cos_t[cslice].T, (2, 1)))
        sinr = np.ascontiguousarray(np.tile(sin_t[cslice].T, (2, 1)))
        in_maps.append(dict(
            x=np.ascontiguousarray(x[cslice]),
            wq=wq_h, wk=wk_h, wv=wv_h, wo=wo_h, we=we_h, wg=wg_h, gw=gw_h,
            bg=np.ascontiguousarray(bg.reshape(24, 1)),
            boutbc=boutbc, nwc=nwc_h, cosr=cosr, sinr=sinr,
            tri=tri_h, ident=ident_h, ones1=ones1_h,
            selc=sel, onems=onems, selrow=selrow,
            epsc=np.ascontiguousarray(np.broadcast_to(
                np.array([1.0 / 1024.0, 1e-5, 1e-6, 0.0], f32)[None, :], (128, 4))),
            esel=np.ascontiguousarray(
                np.repeat(np.eye(16, dtype=f32), 128, axis=1)),
        ))
    return in_maps


def kernel(**inputs):
    in_maps = _host_inputs(inputs)
    nc = build_program()
    res = run_bass_kernel_spmd(nc, in_maps, core_ids=list(range(NCORES)))
    out = np.concatenate([np.asarray(res.results[c]["out"]) for c in range(NCORES)],
                         axis=0).reshape(1, L, HID).astype(np.float32)
    return out


# version 16
# speedup vs baseline: 1.0116x; 1.0116x over previous
import contextlib
import numpy as np
import concourse.bass as bass
import concourse.mybir as mybir
from concourse.bass_utils import run_bass_kernel_spmd

F32 = mybir.dt.float32
BF16 = mybir.dt.bfloat16
AF = mybir.ActivationFunctionType
ALU = mybir.AluOpType
AX = mybir.AxisListType

NCORES = 8
L = 16384
HID = 1024
NH = 8
T = L // NCORES
NSC = 4
NCH = 16
SCALE = float(1.0 / np.sqrt(128.0))
ENGS = ("sync", "vector", "scalar", "tensor", "gpsimd")

AFC = 21504
ABC = 40960


def build_program(debug=False):
    nc = bass.Bass("TRN2", target_bir_lowering=False, debug=False)

    x_in = nc.dram_tensor("x", [T, HID], F32, kind="ExternalInput").ap()
    wq_in = nc.dram_tensor("wq", [8, 128, 1024], F32, kind="ExternalInput").ap()
    wk_in = nc.dram_tensor("wk", [8, 128, 1024], F32, kind="ExternalInput").ap()
    wv_in = nc.dram_tensor("wv", [128, 8192], F32, kind="ExternalInput").ap()
    wo_in = nc.dram_tensor("wo", [128, 8192], F32, kind="ExternalInput").ap()
    we_in = nc.dram_tensor("we", [8, 128, 8192], F32, kind="ExternalInput").ap()
    wg_in = nc.dram_tensor("wg", [128, 192], F32, kind="ExternalInput").ap()
    gw_in = nc.dram_tensor("gw", [128, 64], F32, kind="ExternalInput").ap()
    bg_in = nc.dram_tensor("bg", [24, 1], F32, kind="ExternalInput").ap()
    boutbc_in = nc.dram_tensor("boutbc", [128, HID], F32, kind="ExternalInput").ap()
    nwc_in = nc.dram_tensor("nwc", [128, 8], F32, kind="ExternalInput").ap()
    cos_in = nc.dram_tensor("cosr", [128, T], F32, kind="ExternalInput").ap()
    sin_in = nc.dram_tensor("sinr", [128, T], F32, kind="ExternalInput").ap()
    tri_in = nc.dram_tensor("tri", [128, 128], F32, kind="ExternalInput").ap()
    ident_in = nc.dram_tensor("ident", [128, 128], F32, kind="ExternalInput").ap()
    ones1_in = nc.dram_tensor("ones1", [1, 128], F32, kind="ExternalInput").ap()
    sel_in = nc.dram_tensor("selc", [8, 1], F32, kind="ExternalInput").ap()
    onems_in = nc.dram_tensor("onems", [8, 8], F32, kind="ExternalInput").ap()
    selrow_in = nc.dram_tensor("selrow", [1, 8], F32, kind="ExternalInput").ap()
    epsc_in = nc.dram_tensor("epsc", [128, 4], F32, kind="ExternalInput").ap()
    esel_in = nc.dram_tensor("esel", [16, 2048], F32, kind="ExternalInput").ap()

    out_ext = nc.dram_tensor("out", [T, HID], F32, kind="ExternalOutput").ap()

    xnth_d = nc.dram_tensor("xnth_d", [NSC, 128, 4096], BF16).ap()
    xntl_d = nc.dram_tensor("xntl_d", [NSC, 128, 4096], BF16).ap()
    kt_d = nc.dram_tensor("kt_d", [NSC, 128, 4096], F32).ap()
    v_d = nc.dram_tensor("v_d", [NSC, 128, 4096], F32).ap()
    u_d = nc.dram_tensor("u_d", [NCH, 128, 1024], F32).ap()
    weh_d = nc.dram_tensor("weh_d", [8, 128, 8192], BF16).ap()
    wel_d = nc.dram_tensor("wel_d", [8, 128, 8192], BF16).ap()
    yth_d = nc.dram_tensor("yth_d", [NSC, 128, 4096], BF16).ap()
    ytl_d = nc.dram_tensor("ytl_d", [NSC, 128, 4096], BF16).ap()
    mex_i = nc.dram_tensor("mex_i", [128, 1024], F32).ap()
    mex_o = nc.dram_tensor("mex_o", [1024, 1024], F32).ap()
    aseg_i = nc.dram_tensor("aseg_i", [1, 8], F32).ap()
    aseg_o = nc.dram_tensor("aseg_o", [8, 8], F32).ap()

    dbg = {}
    if debug:
        for name, shape in [("d_min", [128, 1024]), ("d_mloc", [128, 1024]),
                            ("d_cols", [128, 384]), ("d_abcol", [128, 128]),
                            ("d_aends", [8, 16]), ("d_stats", [128, 48]),
                            ("d_y", [128, 4096]), ("d_onehot", [128, 128]),
                            ("d_ASC", [128, 4096]), ("d_gl", [24, 512]),
                            ("d_zl", [24, 512]), ("d_al", [8, 512]),
                            ("d_sb", [128, 512]), ("d_xnh", [128, 512]),
                            ("d_xnl", [128, 512]), ("d_st0", [128, 1]), ("d_st1", [128, 1]),
                            ("d_st2", [128, 1]), ("d_st3", [128, 1])]:
            dbg[name] = nc.dram_tensor(name, shape, F32, kind="ExternalOutput").ap()

    af_h = nc.alloc_sbuf_tensor("arena_f", [128, AFC], F32)
    ab_h = nc.alloc_sbuf_tensor("arena_b", [128, ABC], BF16)

    def AFV(off, wdt, parts=128, p0=0):
        return af_h[p0:p0 + parts, off:off + wdt]

    def ABV(off, wdt, parts=128, p0=0):
        return ab_h[p0:p0 + parts, off:off + wdt]

    sbt = lambda n, s, d=F32: nc.alloc_sbuf_tensor(n, s, d)
    cols = sbt("cols", [128, NCH * 24])
    abcol = sbt("abcol", [128, NCH * 8])
    aends = sbt("aends", [8, NCH])
    ascan = sbt("ascan", [8, NCH])
    aendT = sbt("aendT", [NCH, 8])
    asegrow = sbt("asegrow", [1, 8])
    asegall = sbt("asegall", [8, 8])
    alpharows = sbt("alpharows", [8, 8])
    albc = sbt("albc", [128, 64])
    selbc = sbt("selbc", [128, 8])
    selsb = sbt("selsb", [8, 1])
    onemsb = sbt("onemsb", [8, 8])
    selrowsb = sbt("selrowsb", [1, 8])
    M_sb = sbt("M_sb", [128, 1024])
    Mloc = sbt("Mloc", [128, 1024])
    Mterm = sbt("Mterm", [128, 128])
    P_sb = sbt("P_sb", [128, 128])
    P_sb2 = sbt("P_sb2", [128, 128])
    sbcast = sbt("sbcast", [128, 512])
    ones1 = sbt("ones1_sb", [1, 128])
    ident = sbt("ident_sb", [128, 128])
    tri = sbt("tri_sb", [128, 128])
    zeros8 = sbt("zeros8", [8, 128])
    zero128 = sbt("zero128", [128, 8])
    rstd_a = sbt("rstd_a", [128, NCH])
    nmr_a = sbt("nmr_a", [128, NCH])
    s_a = sbt("s_a", [128, NCH])
    skT = sbt("skT", [NCH, 128])
    mcol = sbt("mcol", [128, 1])
    mc4 = sbt("mc4", [128, 4])
    ng4 = sbt("ng4", [128, 4])
    c14 = sbt("c14", [128, 4])
    c24 = sbt("c24", [128, 4])
    negm = sbt("negm", [128, 1])
    c1 = sbt("c1", [128, 1])
    c2 = sbt("c2", [128, 1])
    nwcols = sbt("nwcols", [128, 8])
    bgcol = sbt("bgcol", [24, 1])
    wgstage = sbt("wgstage", [128, 192])
    wghi = sbt("wghi", [128, 192], BF16)
    wglo = sbt("wglo", [128, 192], BF16)
    gwstage = sbt("gwstage", [128, 64])
    gwhi = sbt("gwhi", [128, 64], BF16)
    gwlo = sbt("gwlo", [128, 64], BF16)
    onehot = sbt("onehot", [128, NCH * 8])
    scores8 = sbt("scores8", [128, 32])
    sc8a = sbt("sc8a", [128, 8])
    sc8b = sbt("sc8b", [128, 8])
    sc8c = sbt("sc8c", [128, 8])
    mx1 = sbt("mx1", [128, 1])
    epsc = sbt("epsc_sb", [128, 4])
    bnop = sbt("bnop", [128, 4])
    esel = sbt("esel_sb", [16, 2048])
    cossb = sbt("cossb", [128, 512])
    sinsb = sbt("sinsb", [128, 512])

    pb = [nc.alloc_psum_tensor(f"pb{i}", [128, 512], F32) for i in range(8)]
    assert nc.sbuf_bytes_remaining >= 0, f"SBUF over: {nc.sbuf_bytes_remaining}"

    plan = {e: [] for e in ENGS}
    stack = contextlib.ExitStack()
    sems, cnt = {}, {}
    for name in ("bar", "t", "v", "s", "g", "cc"):
        sems[name] = stack.enter_context(nc.semaphore(name))
        cnt[name] = 0
    QR = 4
    qsems, qcnt, qrr = {}, {}, {}
    for name in ("din", "dg", "dst", "o"):
        qsems[name] = [stack.enter_context(nc.semaphore(f"{name}{i}"))
                       for i in range(QR)]
        qcnt[name] = [0] * QR
        qrr[name] = 0

    def emit(eng, fn):
        plan[eng].append(fn)

    def dtok(name):
        return dict(enumerate(qcnt[name]))

    def w(eng, sem, val=None):
        if sem in qsems:
            tok = val if isinstance(val, dict) else dtok(sem)
            for i, v2 in tok.items():
                if v2 > 0:
                    h = qsems[sem][i]
                    emit(eng, lambda e, h=h, v3=v2: e.wait_ge(h, v3))
            return
        val = cnt[sem] if val is None else val
        if val <= 0:
            return
        h = sems[sem]
        emit(eng, lambda e, h=h, v2=val: e.wait_ge(h, v2))

    def inc(eng, sem, by, fn):
        cnt[sem] += by
        h = sems[sem]
        emit(eng, lambda e, h=h, by=by, fn=fn: fn(e).then_inc(h, by))

    def run(eng, fn):
        if eng in OWN:
            inc(eng, OWN[eng], 1, fn)
        else:
            emit(eng, fn)

    def snap():
        return dict(t=cnt["t"], v=cnt["v"], s=cnt["s"], g=cnt["g"])

    def wsnap(eng, sn):
        for k2 in ("t", "v", "s", "g"):
            if sn[k2] > 0:
                w(eng, k2, sn[k2])

    OWN = {"vector": "v", "scalar": "s", "tensor": "t", "gpsimd": "g"}

    def barrier():
        for _ in ENGS:
            cnt["bar"] += 1
        tgt = cnt["bar"]
        h = sems["bar"]
        for eng in ENGS:
            if eng in OWN:
                w(eng, OWN[eng])
            emit(eng, lambda e, h=h: e.sem_inc(h, 1))
            emit(eng, lambda e, h=h, t2=tgt: e.wait_ge(h, t2))

    def dma(eng, sem, out, in_):
        i = qrr[sem]
        qrr[sem] = (i + 1) % QR
        h = qsems[sem][i]
        prev = qcnt[sem][i]
        if prev > 0:
            emit(eng, lambda e, h=h, v2=prev: e.wait_ge(h, v2))
        qcnt[sem][i] = prev + 16
        emit(eng, lambda e, h=h, o=out, i2=in_: e.dma_start(out=o, in_=i2).then_inc(h, 16))

    bank_free = {i: None for i in range(8)}

    def bank_wait(eng, i):
        if bank_free[i] is not None:
            sname, val = bank_free[i]
            w(eng, sname, val)

    ssw_state = {"vdone": {}}

    def stream_split_stationary(win, stage_off, hi_off=8192, lo_off=16384):
        ssw_state["vdone"] = {}
        for mb in range(8):
            slot = AFV(stage_off + (mb % 2) * 1024, 1024)
            if mb >= 2:
                w("sync", "v", ssw_state["vdone"][mb - 2])
            dma("sync", "din", slot, win[mb])
            w("vector", "din")
            for kc in range(8):
                hi = ABV(hi_off + mb * 1024 + kc * 128, 128)
                st = slot[:, kc * 128:(kc + 1) * 128]
                fn = lambda e, hi=hi, st=st, kc=kc: e.tensor_scalar(
                    hi, st, nwcols[:, kc:kc + 1], None, ALU.mult)
                inc("vector", "v", 1, fn)
            w("vector", "v")
            for kc in range(8):
                hi = ABV(hi_off + mb * 1024 + kc * 128, 128)
                lo = ABV(lo_off + mb * 1024 + kc * 128, 128)
                st = slot[:, kc * 128:(kc + 1) * 128]
                fn = lambda e, hi=hi, lo=lo, st=st, kc=kc: e.scalar_tensor_tensor(
                    lo, st, nwcols[:, kc:kc + 1], hi, ALU.mult, ALU.subtract)
                inc("vector", "v", 1, fn)
            ssw_state["vdone"][mb] = cnt["v"]

    def stream_split_moving(win, fold_norm, stage_off):
        for hf in range(2):
            slot = AFV(stage_off, 4096)
            if hf == 1:
                w("sync", "v")
            dma("sync", "din", slot, win[:, hf * 4096:(hf + 1) * 4096])
            w("vector", "din")
            for kc in range(4):
                kca = hf * 4 + kc
                hi = ABV(8192 + hf * 4096 + kc * 1024, 1024)
                st = slot[:, kc * 1024:(kc + 1) * 1024]
                if fold_norm:
                    fn = lambda e, hi=hi, st=st, kca=kca: e.tensor_scalar(
                        hi, st, nwcols[:, kca:kca + 1], None, ALU.mult)
                else:
                    fn = lambda e, hi=hi, st=st: e.tensor_copy(hi, st)
                inc("vector", "v", 1, fn)
            w("vector", "v")
            for kc in range(4):
                kca = hf * 4 + kc
                hi = ABV(8192 + hf * 4096 + kc * 1024, 1024)
                lo = ABV(16384 + hf * 4096 + kc * 1024, 1024)
                st = slot[:, kc * 1024:(kc + 1) * 1024]
                if fold_norm:
                    fn = lambda e, hi=hi, lo=lo, st=st, kca=kca: e.scalar_tensor_tensor(
                        lo, st, nwcols[:, kca:kca + 1], hi, ALU.mult, ALU.subtract)
                else:
                    fn = lambda e, hi=hi, lo=lo, st=st: e.tensor_tensor(
                        lo, st, hi, ALU.subtract)
                inc("vector", "v", 1, fn)

    def build_sbcast(sc):
        bank_wait("tensor", 6)

        def mk(e, sc=sc):
            last = None
            for wj in range(4):
                tc = sc * 4 + wj
                last = e.matmul(pb[6][:, wj * 128:(wj + 1) * 128],
                                esel[:, tc * 128:(tc + 1) * 128],
                                skT[0:16, 0:128], start=True, stop=True)
            return last
        inc("tensor", "t", 1, mk)
        w("scalar", "t")
        w("scalar", "v")
        inc("scalar", "s", 1, lambda e: e.activation(sbcast[:, :], pb[6][:, 0:512], AF.Copy))
        bank_free[6] = ("s", cnt["s"])

    def proj_stationary_weight(dst_base, xbase=(0, 4096), wbase=(8192, 16384)):
        for mb in range(8):
            bi = mb % 2
            bank = pb[bi]
            bank_wait("tensor", bi)

            def mk(e, mb=mb, bank=bank):
                last = None
                for kc in range(8):
                    xh = ABV(xbase[0] + kc * 512, 512)
                    xl = ABV(xbase[1] + kc * 512, 512)
                    wh = ABV(wbase[0] + mb * 1024 + kc * 128, 128)
                    wl = ABV(wbase[1] + mb * 1024 + kc * 128, 128)
                    e.matmul(bank[:, 0:512], wh, xh, start=(kc == 0), stop=False)
                    e.matmul(bank[:, 0:512], wh, xl, start=False, stop=False)
                    last = e.matmul(bank[:, 0:512], wl, xh, start=False, stop=(kc == 7))
                return last
            inc("tensor", "t", 1, mk)
            w("vector", "t")
            w("vector", "s")
            dst = AFV(dst_base + mb * 512, 512)
            inc("vector", "v", 1, lambda e, dst=dst, bank=bank: e.tensor_tensor(
                dst, bank[:, 0:512], sbcast[:, :], ALU.mult))
            bank_free[bi] = ("v", cnt["v"])

    def rope(base, scrA, scrB, eng2="gpsimd"):
        x1 = AFV(base, 2048)
        x2 = AFV(base + 2048, 2048)
        tA = AFV(scrA, 2048)
        tB = AFV(scrB, 2048)
        cosv = cossb.ap()[:, 0:512].unsqueeze(1).broadcast_to([128, 4, 512])
        sinv = sinsb.ap()[:, 0:512].unsqueeze(1).broadcast_to([128, 4, 512])

        def v3(apx):
            return apx.rearrange("p (b f) -> p b f", b=4)
        w("vector", "din")
        run("vector", lambda e: e.tensor_tensor(v3(tA), v3(x1), sinv, ALU.mult))
        inc("vector", "v", 1, lambda e: e.tensor_tensor(v3(x1), v3(x1), cosv, ALU.mult))
        if eng2 == "gpsimd":
            w("gpsimd", "din")
            inc("gpsimd", "g", 1, lambda e: e.tensor_tensor(v3(tB), v3(x2), sinv, ALU.mult))
            inc("gpsimd", "g", 1, lambda e: e.tensor_tensor(v3(x2), v3(x2), cosv, ALU.mult))
            w("vector", "g")
            w("vector", "v")
        else:
            run("vector", lambda e: e.tensor_tensor(v3(tB), v3(x2), sinv, ALU.mult))
            inc("vector", "v", 1, lambda e: e.tensor_tensor(v3(x2), v3(x2), cosv, ALU.mult))
            w("vector", "v")
        run("vector", lambda e: e.tensor_tensor(v3(x1), v3(x1), v3(tB), ALU.subtract))
        inc("vector", "v", 1, lambda e: e.tensor_tensor(v3(x2), v3(x2), v3(tA), ALU.add))


    dma("sync", "din", tri[:, :], tri_in)
    dma("sync", "din", ident[:, :], ident_in)
    dma("sync", "din", ones1[:, :], ones1_in)
    dma("sync", "din", nwcols[:, :], nwc_in)
    dma("sync", "din", bgcol[:, :], bg_in)
    dma("sync", "din", selsb[:, :], sel_in)
    dma("sync", "din", onemsb[:, :], onems_in)
    dma("sync", "din", selrowsb[:, :], selrow_in)
    dma("sync", "din", wgstage[:, :], wg_in)
    dma("sync", "din", gwstage[:, :], gw_in)
    dma("sync", "din", epsc[:, :], epsc_in)
    dma("sync", "din", esel[:, :], esel_in)
    run("vector", lambda e: e.memset(zeros8[:, :], 0.0))
    run("vector", lambda e: e.memset(zero128[:, :], 0.0))
    run("vector", lambda e: e.memset(M_sb[:, :], 0.0))
    inc("vector", "v", 1, lambda e: e.memset(Mloc[:, :], 0.0))
    w("gpsimd", "din")
    for kc in range(8):
        inc("gpsimd", "g", 1, lambda e, kc=kc: e.tensor_scalar(
            wgstage[:, kc * 24:(kc + 1) * 24], wgstage[:, kc * 24:(kc + 1) * 24],
            nwcols[:, kc:kc + 1], None, ALU.mult))
    w("gpsimd", "g")
    inc("gpsimd", "g", 1, lambda e: e.tensor_copy(wghi[:, :], wgstage[:, :]))
    w("gpsimd", "g")
    inc("gpsimd", "g", 1, lambda e: e.tensor_tensor(wglo[:, :], wgstage[:, :], wghi[:, :], ALU.subtract))
    inc("gpsimd", "g", 1, lambda e: e.tensor_copy(gwhi[:, :], gwstage[:, :]))
    w("gpsimd", "g")
    inc("gpsimd", "g", 1, lambda e: e.tensor_tensor(gwlo[:, :], gwstage[:, :], gwhi[:, :], ALU.subtract))

    stream_split_moving(wv_in, True, 10240)
    wv_done = snap()

    p0_xv = p0_xs = p0_t_prev = p0_spill_prev = None
    for sc in range(NSC):
        if sc > 0:
            w("sync", "v", p0_xv)
            w("sync", "s", p0_xs)
        for wj in range(4):
            tc = sc * 4 + wj
            dma("sync", "din", AFV(wj * 1024, 1024), x_in[tc * 128:(tc + 1) * 128, :])
        w("vector", "din")
        for wj in range(4):
            fn = lambda e, wj=wj: e.reduce_sum(mc4[:, wj:wj + 1], AFV(wj * 1024, 1024), axis=AX.X)
            inc("vector", "v", 1, fn)
        w("vector", "v")
        for wj in range(4):
            fn = lambda e, wj=wj: e.tensor_scalar(
                ng4[:, wj:wj + 1], mc4[:, wj:wj + 1], -1.0 / 1024.0, None, ALU.mult)
            inc("vector", "v", 1, fn)
        w("scalar", "v")
        for wj in range(4):
            fn = lambda e, wj=wj: e.activation(
                AFV(16384 + wj * 1024, 1024), AFV(wj * 1024, 1024), AF.Square,
                bias=ng4[:, wj:wj + 1], scale=1.0, accum_out=c14[:, wj:wj + 1])
            inc("scalar", "s", 1, fn)
        p0_xs = cnt["s"]
        w("scalar", "s")
        for wj in range(4):
            fn = lambda e, wj=wj: e.activation(
                c24[:, wj:wj + 1], c14[:, wj:wj + 1], AF.Sqrt,
                bias=epsc[:, 1:2], scale=epsc[:, 0:1])
            inc("scalar", "s", 1, fn)
        w("vector", "s")
        for wj in range(4):
            tc = sc * 4 + wj
            fn = lambda e, wj=wj, tc=tc: e.reciprocal(rstd_a[:, tc:tc + 1], c24[:, wj:wj + 1])
            inc("vector", "v", 1, fn)
        w("vector", "v")
        for wj in range(4):
            tc = sc * 4 + wj
            fn = lambda e, wj=wj, tc=tc: e.tensor_mul(
                nmr_a[:, tc:tc + 1], ng4[:, wj:wj + 1], rstd_a[:, tc:tc + 1])
            inc("vector", "v", 1, fn)
        w("vector", "v")
        if sc > 0:
            w("vector", "t", p0_t_prev)
        for wj in range(4):
            tc = sc * 4 + wj
            fn = lambda e, wj=wj, tc=tc: e.tensor_scalar(
                AFV(5120 + wj * 1024, 1024), AFV(wj * 1024, 1024),
                rstd_a[:, tc:tc + 1], nmr_a[:, tc:tc + 1], ALU.mult, ALU.add)
            inc("vector", "v", 1, fn)
        p0_xv = cnt["v"]
        w("scalar", "v")
        for wj in range(4):
            fn = lambda e, wj=wj: e.activation(
                AFV(16384 + wj * 1024, 1024), AFV(5120 + wj * 1024, 1024), AF.Square,
                accum_out=c14[:, wj:wj + 1])
            inc("scalar", "s", 1, fn)
        w("scalar", "s")
        for wj in range(4):
            fn = lambda e, wj=wj: e.activation(
                c24[:, wj:wj + 1], c14[:, wj:wj + 1], AF.Sqrt,
                bias=epsc[:, 2:3], scale=epsc[:, 0:1])
            inc("scalar", "s", 1, fn)
        w("vector", "s")
        for wj in range(4):
            tc = sc * 4 + wj
            fn = lambda e, wj=wj, tc=tc: e.reciprocal(s_a[:, tc:tc + 1], c24[:, wj:wj + 1])
            inc("vector", "v", 1, fn)
        sn = snap()
        wsnap("tensor", sn)
        if sc > 0:
            w("scalar", "dst", p0_spill_prev)
        for kc in range(8):
            bi = kc % 2
            bank = pb[bi]
            bank_wait("tensor", bi)

            def mk_tr(e, kc=kc, bank=bank):
                last = None
                for wj in range(4):
                    xn4 = AFV(5120 + wj * 1024, 1024)
                    last = e.transpose(bank[:, wj * 128:(wj + 1) * 128],
                                       xn4[:, kc * 128:(kc + 1) * 128], ident[:, :])
                return last
            inc("tensor", "t", 1, mk_tr)
            tq = cnt["t"]
            hi = ABV(kc * 512, 512)
            lo = ABV(4096 + kc * 512, 512)
            w("scalar", "t", tq)
            inc("scalar", "s", 1, lambda e, hi=hi, bank=bank: e.activation(
                hi, bank[:, 0:512], AF.Copy))
            w("vector", "s")
            inc("vector", "v", 1, lambda e, lo=lo, hi=hi, bank=bank: e.tensor_tensor(
                lo, bank[:, 0:512], hi, ALU.subtract))
            bank_free[bi] = ("v", cnt["v"])
        p0_t_prev = cnt["t"]
        w("gpsimd", "v")
        dma("gpsimd", "dst", xnth_d[sc], ABV(0, 4096))
        dma("gpsimd", "dst", xntl_d[sc], ABV(4096, 4096))
        p0_spill_prev = dtok("dst")
        w("gpsimd", "dst")

    if debug:
        w("gpsimd", "v")
        w("gpsimd", "s")
        dma("gpsimd", "o", dbg["d_st0"], mcol[:, :])
        dma("gpsimd", "o", dbg["d_st1"], negm[:, :])
        dma("gpsimd", "o", dbg["d_st2"], c1[:, :])
        dma("gpsimd", "o", dbg["d_st3"], c2[:, :])
    bank_wait("tensor", 2)
    w("tensor", "v")
    inc("tensor", "t", 1, lambda e: e.transpose(pb[2][0:16, 0:128], s_a[:, :], ident[:, :]))
    w("vector", "t")
    inc("vector", "v", 1, lambda e: e.tensor_copy(skT[:, :], pb[2][0:16, 0:128]))
    bank_free[2] = ("v", cnt["v"])
    barrier()

    stream_split_stationary(wk_in, 12288, hi_off=24576, lo_off=32768)
    wk_done = snap()
    AL = AFV(16384, 512, parts=8)
    GL24 = AFV(16896, 512, parts=24)
    ZL = AFV(17408, 512, parts=24)
    RW = AFV(17920, 512, parts=8)
    RAO = AFV(18432, 512, parts=8)
    RK = AFV(18944, 512, parts=8)
    G8 = AFV(19456, 512, parts=8)
    O8 = AFV(19968, 512, parts=8)
    aendT4 = aendT[0:4, 0:8]
    for sc in range(NSC):
        scs = slice(sc * 512, (sc + 1) * 512)
        dma("gpsimd", "dg", ABV(0, 4096), xnth_d[sc])
        dma("gpsimd", "dg", ABV(4096, 4096), xntl_d[sc])
        dma("sync", "din", cossb[:, :], cos_in[:, scs])
        dma("sync", "din", sinsb[:, :], sin_in[:, scs])
        w("tensor", "dg")
        build_sbcast(sc)
        bank_wait("tensor", 7)

        def mk_g(e):
            last = None
            for wj in range(4):
                for kc in range(8):
                    xh = ABV(kc * 512 + wj * 128, 128)
                    xl = ABV(4096 + kc * 512 + wj * 128, 128)
                    wh = wghi[:, kc * 24:(kc + 1) * 24]
                    wl = wglo[:, kc * 24:(kc + 1) * 24]
                    reg = pb[7][0:24, wj * 128:wj * 128 + 128]
                    e.matmul(reg, wh, xh, start=(kc == 0), stop=False)
                    e.matmul(reg, wh, xl, start=False, stop=False)
                    last = e.matmul(reg, wl, xh, start=False, stop=(kc == 7))
            return last
        inc("tensor", "t", 1, mk_g)
        w("vector", "t")
        w("vector", "s")

        def mk_zs(e, ZL=ZL):
            last = None
            for wj in range(4):
                last = e.tensor_tensor(ZL[0:24, wj * 128:(wj + 1) * 128],
                                       pb[7][0:24, wj * 128:wj * 128 + 128],
                                       sbcast[0:24, wj * 128:(wj + 1) * 128], ALU.mult)
            return last
        inc("vector", "v", 1, mk_zs)
        bank_free[7] = ("v", cnt["v"])
        w("scalar", "v")
        inc("scalar", "s", 1, lambda e, GL24=GL24, ZL=ZL: e.activation(
            GL24, ZL, AF.Sigmoid, bias=bgcol[:, 0:1]))
        w("gpsimd", "s")
        dma("gpsimd", "dg", G8, GL24[8:16, 0:512])
        dma("gpsimd", "dg", O8, GL24[16:24, 0:512])
        w("vector", "s")
        for wj in range(4):
            lw = wj * 128
            inc("vector", "v", 1, lambda e, lw=lw, GL24=GL24, AL=AL: e.tensor_tensor_scan(
                AL[0:8, lw:lw + 128], GL24[0:8, lw:lw + 128],
                zeros8[0:8, 0:128], 1.0, ALU.mult, ALU.add))
        w("vector", "v")
        for wj in range(4):
            tc = sc * 4 + wj
            lw = wj * 128
            run("vector", lambda e, tc=tc, lw=lw, AL=AL: e.tensor_copy(
                aends[0:8, tc:tc + 1], AL[0:8, lw + 127:lw + 128]))
        inc("vector", "v", 1, lambda e, RW=RW, AL=AL: e.reciprocal(RW[0:8, :], AL[0:8, :]))
        w("vector", "v")
        inc("vector", "v", 1, lambda e, RW=RW: e.tensor_scalar(
            RW[0:8, :], RW[0:8, :], SCALE, None, ALU.mult))
        w("vector", "v")
        w("vector", "dg")
        inc("vector", "v", 1, lambda e, RW=RW, G8=G8: e.tensor_tensor(
            RW[0:8, :], RW[0:8, :], G8[0:8, :], ALU.mult))
        w("vector", "v")
        run("vector", lambda e, RAO=RAO, O8=O8, AL=AL: e.tensor_tensor(
            RAO[0:8, :], AL[0:8, :], O8[0:8, :], ALU.mult))
        for wj in range(4):
            tc = sc * 4 + wj
            lw = wj * 128
            run("vector", lambda e, lw=lw, tc=tc, RW=RW, RK=RK: e.tensor_scalar(
                RK[0:8, lw:lw + 128], RW[0:8, lw:lw + 128],
                aends[0:8, tc:tc + 1], None, ALU.mult))
        inc("vector", "v", 1, lambda e: e.engine_nop())
        sn = snap()
        wsnap("tensor", sn)
        bank_wait("tensor", 6)

        def mk_cols(e, sc=sc, RW=RW, RAO=RAO, RK=RK):
            last = None
            for wj in range(4):
                lw = wj * 128
                for bd, RB in ((0, RW), (8, RAO), (16, RK)):
                    last = e.transpose(pb[6][:, wj * 32 + bd:wj * 32 + bd + 8],
                                       RB[0:8, lw:lw + 128], ident[0:8, 0:8])
            return last
        inc("tensor", "t", 1, mk_cols)
        w("scalar", "t")

        def mk_cols_ev(e, sc=sc):
            last = None
            for wj in range(4):
                tc = sc * 4 + wj
                last = e.activation(cols[:, tc * 24:(tc + 1) * 24],
                                    pb[6][:, wj * 32:wj * 32 + 24], AF.Copy)
            return last
        inc("scalar", "s", 1, mk_cols_ev)
        bank_free[6] = ("s", cnt["s"])
        bank_wait("tensor", 2)
        inc("tensor", "t", 1, lambda e, sc=sc: e.transpose(
            pb[2][0:4, 0:8], aends[0:8, sc * 4:(sc + 1) * 4], ident[0:8, 0:8]))
        w("vector", "t")
        inc("vector", "v", 1, lambda e: e.tensor_copy(aendT4, pb[2][0:4, 0:8]))
        bank_free[2] = ("v", cnt["v"])
        w("tensor", "v")
        bank_wait("tensor", 2)

        def mk_ab(e, sc=sc):
            last = None
            for wj in range(4):
                last = e.matmul(pb[2][:, wj * 8:(wj + 1) * 8],
                                esel[0:4, wj * 128:(wj + 1) * 128],
                                aendT4, start=True, stop=True)
            return last
        inc("tensor", "t", 1, mk_ab)
        w("scalar", "t")
        inc("scalar", "s", 1, lambda e, sc=sc: e.activation(
            abcol[:, sc * 32:(sc + 1) * 32], pb[2][:, 0:32], AF.Copy))
        bank_free[2] = ("s", cnt["s"])
        wsnap("tensor", wv_done)
        for fh in range(2):
            for wj in range(4):
                tc = sc * 4 + wj
                bi = (fh * 4 + wj) % 2
                bank = pb[bi]
                bank_wait("tensor", bi)

                def mk_v(e, fh=fh, wj=wj, bank=bank):
                    last = None
                    for kc in range(8):
                        xh = ABV(kc * 512 + wj * 128, 128)
                        xl = ABV(4096 + kc * 512 + wj * 128, 128)
                        wh = ABV(8192 + kc * 1024 + fh * 512, 512)
                        wl = ABV(16384 + kc * 1024 + fh * 512, 512)
                        e.matmul(bank[:, 0:512], xh, wh, start=(kc == 0), stop=False)
                        e.matmul(bank[:, 0:512], xh, wl, start=False, stop=False)
                        last = e.matmul(bank[:, 0:512], xl, wh, start=False, stop=(kc == 7))
                    return last
                inc("tensor", "t", 1, mk_v)
                w("vector", "t")
                dst = AFV(wj * 1024 + fh * 512, 512)
                inc("vector", "v", 1, lambda e, dst=dst, bank=bank, tc=tc: e.tensor_scalar(
                    dst, bank[:, 0:512], s_a[:, tc:tc + 1], None, ALU.mult))
                bank_free[bi] = ("v", cnt["v"])
        wsnap("tensor", wk_done)
        proj_stationary_weight(4096, wbase=(24576, 32768))
        sn = snap()
        wsnap("vector", sn)
        wsnap("gpsimd", sn)
        rope(4096, 8192, 10240)
        rope_done = snap()
        for wj in range(4):
            tc = sc * 4 + wj
            for grp in range(2):
                bi = 2 + grp
                bank = pb[bi]
                bank_wait("tensor", bi)
                wsnap("tensor", rope_done)

                def mk_ktr(e, wj=wj, grp=grp, bank=bank):
                    last = None
                    for fq in range(4):
                        fc = grp * 4 + fq
                        src = AFV(4096 + fc * 512 + wj * 128, 128)
                        last = e.transpose(bank[:, fq * 128:(fq + 1) * 128],
                                           src, ident[:, :])
                    return last
                inc("tensor", "t", 1, mk_ktr)
                tq = cnt["t"]
                w("vector", "t", tq)
                for fq in range(4):
                    for sub in range(2):
                        h = 2 * fq + sub
                        d_off = 15360 + h * 128 + grp * 64
                        s_c = fq * 128 + sub * 64
                        ccol = cols[:, tc * 24 + 16 + h:tc * 24 + 17 + h]
                        inc("vector", "v", 1, lambda e, d=d_off, s2=s_c, bank=bank, ccol=ccol: e.tensor_scalar(
                            AFV(d, 64), bank[:, s2:s2 + 64], ccol[:, 0:1], None, ALU.mult))
                bank_free[bi] = ("v", cnt["v"])
            sn2 = snap()
            wsnap("tensor", sn2)
            for half in range(2):
                bi = 4 + half
                bank = pb[bi]
                bank_wait("tensor", bi)

                def mk_u(e, wj=wj, half=half, bank=bank):
                    last = None
                    for h in range(8):
                        b_h = (h % 2) * 64
                        kp = AFV(15360 + h * 128 + half * 64, 64)
                        vv = AFV(wj * 1024 + h * 128, 128)
                        reg = bank[b_h:b_h + 64,
                                   (h // 2) * 128:(h // 2) * 128 + 128]
                        last = e.matmul(reg, kp, vv, start=True, stop=True,
                                        tile_position=(0, b_h))
                    return last
                inc("tensor", "t", 1, mk_u)
                w("scalar", "t")
                inc("scalar", "s", 1, lambda e, half=half, bank=bank: e.activation(
                    AFV(14336 + half * 512, 512), bank[:, 0:512], AF.Copy))
                bank_free[bi] = ("s", cnt["s"])
            w("vector", "s")
            for h in range(8):
                b_h = (h % 2) * 64
                co = (h // 2) * 128

                def mviews2(h=h, b_h=b_h, co=co):
                    mv = Mloc.ap()[b_h:b_h + 64, :].rearrange(
                        "p (two rest) -> p two rest", two=2)[:, :, co:co + 128]
                    uv = AFV(14336, 1024, parts=64, p0=b_h).rearrange(
                        "p (two rest) -> p two rest", two=2)[:, :, co:co + 128]
                    return mv, uv
                run("vector", lambda e, h=h, tc=tc, b_h=b_h, mviews2=mviews2: (
                    lambda mv_uv: e.scalar_tensor_tensor(
                        mv_uv[0], mv_uv[0],
                        abcol[b_h:b_h + 64, tc * 8 + h:tc * 8 + h + 1],
                        mv_uv[1], ALU.mult, ALU.add))(mviews2()))
            inc("vector", "v", 1, lambda e: e.engine_nop())
            w("gpsimd", "v")
            dma("gpsimd", "dst", u_d[tc], AFV(14336, 1024))
            w("scalar", "dst")
        sn3 = snap()
        wsnap("gpsimd", sn3)
        dma("gpsimd", "dst", kt_d[sc], AFV(4096, 4096))
        dma("gpsimd", "dst", v_d[sc], AFV(0, 4096))
        if debug and sc == 3:
            dma("gpsimd", "o", dbg["d_xnh"], ABV(0, 512))
            dma("gpsimd", "o", dbg["d_xnl"], ABV(4096, 512))
            dma("gpsimd", "o", dbg["d_gl"], GL24)
            dma("gpsimd", "o", dbg["d_zl"], ZL)
            dma("gpsimd", "o", dbg["d_al"], AL)
            dma("gpsimd", "o", dbg["d_sb"], sbcast[:, :])
        w("gpsimd", "dst")
        barrier()

    dma("gpsimd", "dst", mex_i, Mloc[:, :])
    inc("vector", "v", 1, lambda e: e.tensor_tensor_scan(
        ascan[:, :], aends[:, :], zeros8[0:8, 0:16], 1.0, ALU.mult, ALU.add))
    w("tensor", "v")
    bank_wait("tensor", 2)
    inc("tensor", "t", 1, lambda e: e.transpose(
        pb[2][0:1, 0:8], ascan[0:8, 15:16], ident[0:8, 0:8]))
    w("scalar", "t")
    inc("scalar", "s", 1, lambda e: e.activation(asegrow[:, :], pb[2][0:1, 0:8], AF.Copy))
    bank_free[2] = ("s", cnt["s"])
    w("gpsimd", "s")
    dma("gpsimd", "dst", aseg_i, asegrow[:, :])
    w("gpsimd", "dst")
    inc("gpsimd", "cc", 1, lambda e: e.collective_compute(
        "AllGather", ALU.bypass, replica_groups=[list(range(NCORES))],
        ins=[mex_i.opt()], outs=[mex_o.opt()]))
    inc("gpsimd", "cc", 1, lambda e: e.collective_compute(
        "AllGather", ALU.bypass, replica_groups=[list(range(NCORES))],
        ins=[aseg_i.opt()], outs=[aseg_o.opt()]))
    stream_split_stationary(wq_in, 16384)
    px_wq_done = snap()

    wq_done = snap()
    yev_state = {}

    def emit_y(h, wj, tc, pev, mups):
        ybi = 3 if h < 4 else 4
        ybank = pb[ybi]
        if h % 4 == 0:
            bank_wait("tensor", ybi)
        w("tensor", "v", pev[h])

        def mk_y(e, h=h, wj=wj, ybank=ybank):
            fc1 = h // 2
            fc2 = 4 + h // 2
            p0 = (h % 2) * 64
            qx1 = AFV(fc1 * 512 + wj * 128, 128, parts=64, p0=p0)
            qx2 = AFV(fc2 * 512 + wj * 128, 128, parts=64, p0=p0)
            vv = AFV(8192 + wj * 1024 + h * 128, 128)
            b_h = (h % 2) * 64
            co = (h // 2) * 128
            reg = ybank[:, (h % 4) * 128:(h % 4 + 1) * 128]
            e.matmul(reg, Pbuf[h % 2][:, :], vv, start=True, stop=False)
            e.matmul(reg, qx1, M_sb[b_h:b_h + 64, co:co + 128],
                     start=False, stop=False)
            return e.matmul(reg, qx2, M_sb[b_h:b_h + 64, 512 + co:512 + co + 128],
                            start=False, stop=True)
        inc("tensor", "t", 1, mk_y)
        if h % 4 == 3:
            tq = cnt["t"]
            w("vector", "t", tq)

            def mk_yev(e, wj=wj, h=h, ybank=ybank, tc=tc):
                last = None
                for hq in range(4):
                    hh = (h // 4) * 4 + hq
                    last = e.tensor_scalar(
                        AFV(12288 + wj * 1024 + hh * 128, 128),
                        ybank[:, hq * 128:(hq + 1) * 128],
                        cols[:, tc * 24 + 8 + hh:tc * 24 + 9 + hh],
                        None, ALU.mult)
                return last
            inc("vector", "v", 1, mk_yev)
            bank_free[ybi] = ("v", cnt["v"])

    Pbuf = [P_sb, P_sb2]
    pb_xslot = [(0, 4096), (32768, 36864)]
    pb_xn_dg = {}

    def pb_xn_prefetch(sc):
        par = sc % 2
        dma("gpsimd", "dg", ABV(pb_xslot[par][0], 4096), xnth_d[sc])
        dma("gpsimd", "dg", ABV(pb_xslot[par][1], 4096), xntl_d[sc])
        pb_xn_dg[sc] = ("dg", dtok("dg"))

    pb_rope_done = {"v": 0, "g": 0}
    pb_kv_dg = {}

    def pb_head(sc, hoisted=False):
        par = sc % 2
        scs = slice(sc * 512, (sc + 1) * 512)
        if hoisted:
            dma("sync", "din", ABV(pb_xslot[par][0], 4096), xnth_d[sc])
            dma("sync", "din", ABV(pb_xslot[par][1], 4096), xntl_d[sc])
            pb_xn_dg[sc] = ("din", dtok("din"))
            dma("sync", "din", AFV(4096, 4096), kt_d[sc])
            dma("sync", "din", AFV(8192, 4096), v_d[sc])
            pb_kv_dg[sc] = ("din", dtok("din"))
        else:
            w("gpsimd", "t")
            dma("gpsimd", "dg", AFV(4096, 4096), kt_d[sc])
            dma("gpsimd", "dg", AFV(8192, 4096), v_d[sc])
            pb_kv_dg[sc] = ("dg", dtok("dg"))
            w("sync", "v", pb_rope_done["v"])
            w("sync", "g", pb_rope_done["g"])
        dma("sync", "din", cossb[:, :], cos_in[:, scs])
        dma("sync", "din", sinsb[:, :], sin_in[:, scs])
        sname, sval = pb_xn_dg[sc]
        w("tensor", sname, sval)
        wsnap("tensor", wq_done)
        build_sbcast(sc)
        proj_stationary_weight(0, xbase=pb_xslot[par])
        if sc + 1 < NSC:
            w("gpsimd", "t")
            pb_xn_prefetch(sc + 1)
        sn = snap()
        wsnap("vector", sn)
        wsnap("gpsimd", sn)
        if sc > 0:
            w("vector", "t")
            w("gpsimd", "t")
        rope(0, 12288, 14336, eng2=("vector" if hoisted else "gpsimd"))
        rope_done = snap()
        pb_rope_done["v"] = cnt["v"]
        pb_rope_done["g"] = cnt["g"]
        wsnap("tensor", rope_done)
        sname, sval = pb_kv_dg[sc]
        w("tensor", sname, sval)

    pb_head(0, hoisted=True)
    w("gpsimd", "cc")
    dma("gpsimd", "dg", asegall[:, :], aseg_o)
    w("vector", "dg")
    inc("vector", "v", 1, lambda e: e.scalar_tensor_tensor(
        alpharows[0:8, :], asegall[0:8, :], selsb[0:8, 0:1],
        onemsb[0:8, :], ALU.mult, ALU.add))
    w("tensor", "v")
    bank_wait("tensor", 2)

    def mk_albc(e):
        for j in range(7):
            e.matmul(pb[2][:, j * 8:(j + 1) * 8],
                     esel[0:8, j * 128:(j + 1) * 128],
                     alpharows[0:8, 0:8], start=True, stop=True)
        return e.matmul(pb[2][:, 56:64], ones1[:, :], selrowsb[0:1, 0:8],
                        start=True, stop=True)
    inc("tensor", "t", 1, mk_albc)
    w("scalar", "t")
    run("scalar", lambda e: e.activation(albc[:, 0:56], pb[2][:, 0:56], AF.Copy))
    inc("scalar", "s", 1, lambda e: e.activation(selbc[:, 0:8], pb[2][:, 56:64], AF.Copy))
    bank_free[2] = ("s", cnt["s"])
    w("vector", "s")
    Mtmp = AFV(16384, 1024)
    Mterm2 = AFV(17408, 1024)
    for j in range(7):
        w("gpsimd", "v")
        dma("gpsimd", "dg", Mtmp, mex_o[j * 128:(j + 1) * 128, :])
        w("vector", "dg")
        run("vector", lambda e, j=j, Mtmp=Mtmp: e.tensor_scalar(
            Mterm2[:, :], Mtmp[:, :], selbc[:, j:j + 1], None, ALU.mult))
        w("vector", "v")
        for h in range(8):
            b_h = (h % 2) * 64
            for half in range(2):
                co = half * 512 + (h // 2) * 128
                run("vector", lambda e, h=h, j=j, b_h=b_h, co=co: e.scalar_tensor_tensor(
                    M_sb[b_h:b_h + 64, co:co + 128], M_sb[b_h:b_h + 64, co:co + 128],
                    albc[b_h:b_h + 64, j * 8 + h:j * 8 + h + 1],
                    Mterm2[b_h:b_h + 64, co:co + 128], ALU.mult, ALU.add))
        inc("vector", "v", 1, lambda e: e.engine_nop())
    if debug:
        w("gpsimd", "v")
        dma("gpsimd", "o", dbg["d_min"], M_sb[:, :])
        dma("gpsimd", "o", dbg["d_mloc"], Mloc[:, :])
        dma("gpsimd", "o", dbg["d_cols"], cols[:, :])
        dma("gpsimd", "o", dbg["d_abcol"], abcol[:, :])
        dma("gpsimd", "o", dbg["d_aends"], aends[:, :])
        dma("gpsimd", "o", dbg["d_stats"][:, 0:16], rstd_a[:, :])
        dma("gpsimd", "o", dbg["d_stats"][:, 16:32], nmr_a[:, :])
        dma("gpsimd", "o", dbg["d_stats"][:, 32:48], s_a[:, :])
    barrier()


    for sc in range(NSC):
        par = sc % 2
        if sc > 0:
            pb_head(sc)
        for wj in range(4):
            tc = sc * 4 + wj
            w("gpsimd", "v")
            dma("gpsimd", "dg", AFV(16384, 1024), u_d[tc])
            udg = dtok("dg")
            mups = 0
            pev = {}
            for h in range(8):
                sbi = 2 if (h % 2 == 0) else 5
                sbank = pb[sbi]
                bank_wait("tensor", sbi)

                def mk_s(e, h=h, wj=wj, sbank=sbank):
                    fc1 = h // 2
                    fc2 = 4 + h // 2
                    p0 = (h % 2) * 64
                    kx1 = AFV(4096 + fc1 * 512 + wj * 128, 128, parts=64, p0=p0)
                    kx2 = AFV(4096 + fc2 * 512 + wj * 128, 128, parts=64, p0=p0)
                    qx1 = AFV(fc1 * 512 + wj * 128, 128, parts=64, p0=p0)
                    qx2 = AFV(fc2 * 512 + wj * 128, 128, parts=64, p0=p0)
                    e.matmul(sbank[:, 0:128], kx1, qx1, start=True, stop=False)
                    return e.matmul(sbank[:, 0:128], kx2, qx2, start=False, stop=True)
                inc("tensor", "t", 1, mk_s)
                w("vector", "t")
                inc("vector", "v", 1, lambda e, sbank=sbank, tc=tc, h=h: e.scalar_tensor_tensor(
                    Pbuf[h % 2][:, :], sbank[:, 0:128],
                    cols[:, tc * 24 + h:tc * 24 + h + 1], tri[:, :],
                    ALU.mult, ALU.mult))
                pev[h] = cnt["v"]
                bank_free[sbi] = ("v", cnt["v"])
                if h >= 1:
                    emit_y(h - 1, wj, tc, pev, mups)
            emit_y(7, wj, tc, pev, mups)
            w("vector", "dg", udg)
            w("vector", "t")
            for h in range(8):
                b_h = (h % 2) * 64
                co = (h // 2) * 128

                def mviews(h=h, b_h=b_h, co=co):
                    mv = M_sb.ap()[b_h:b_h + 64, :].rearrange(
                        "p (two rest) -> p two rest", two=2)[:, :, co:co + 128]
                    uv = AFV(16384, 1024, parts=64, p0=b_h).rearrange(
                        "p (two rest) -> p two rest", two=2)[:, :, co:co + 128]
                    return mv, uv
                fn = lambda e, h=h, tc=tc, b_h=b_h, mviews=mviews: (
                    lambda mv_uv: e.scalar_tensor_tensor(
                        mv_uv[0], mv_uv[0],
                        abcol[b_h:b_h + 64, tc * 8 + h:tc * 8 + h + 1],
                        mv_uv[1], ALU.mult, ALU.add))(mviews())
                if h == 7:
                    inc("vector", "v", 1, fn)
                else:
                    run("vector", fn)
        if debug and sc == 0:
            w("gpsimd", "v")
            dma("gpsimd", "o", dbg["d_y"], AFV(12288, 4096))
        sn = snap()
        wsnap("tensor", sn)
        for kc in range(8):
            bi = 6 + (kc % 2)
            bank = pb[bi]
            bank_wait("tensor", bi)

            def mk_ytr(e, kc=kc, bank=bank):
                last = None
                for wj in range(4):
                    src = AFV(12288 + wj * 1024 + kc * 128, 128)
                    last = e.transpose(bank[:, wj * 128:(wj + 1) * 128], src, ident[:, :])
                return last
            inc("tensor", "t", 1, mk_ytr)
            w("scalar", "t")
            if sc > 0 and kc == 0:
                w("scalar", "dst", pb_spill[0])
                w("vector", "dst", pb_spill[0])
            hi = ABV(24576 + kc * 512, 512)
            lo = ABV(28672 + kc * 512, 512)
            inc("scalar", "s", 1, lambda e, hi=hi, bank=bank: e.activation(
                hi, bank[:, 0:512], AF.Copy))
            w("vector", "s")
            inc("vector", "v", 1, lambda e, lo=lo, hi=hi, bank=bank: e.tensor_tensor(
                lo, bank[:, 0:512], hi, ALU.subtract))
            bank_free[bi] = ("v", cnt["v"])
        w("gpsimd", "v")
        dma("gpsimd", "dst", yth_d[sc], ABV(24576, 4096))
        dma("gpsimd", "dst", ytl_d[sc], ABV(28672, 4096))
        pb_spill = (dtok("dst"),)
    w("gpsimd", "dst")
    barrier()

    stream_split_moving(wo_in, False, 16384)
    wo_done = snap()
    wo_stage_free = dict(s=cnt["s"], v=cnt["v"])
    dma("sync", "din", AFV(20480, 1024), boutbc_in)
    boutd = dtok("din")
    w("vector", "din", boutd)

    presplit_pieces = [(e_i, pc2) for e_i in range(8) for pc2 in range(4)]
    pre_state = {"i": 0, "dst_done": 0}

    def emit_presplit(n):
        for _ in range(n):
            i = pre_state["i"]
            if i >= len(presplit_pieces):
                return
            pre_state["i"] += 1
            e_i, pc2 = presplit_pieces[i]
            stage = AFV(16384, 2048)
            lo_bf = AFV(19456, 1024).bitcast(BF16)
            if i == 0:
                w("sync", "v", wo_stage_free["v"])
            else:
                w("sync", "dst")
            dma("sync", "din", stage, we_in[e_i][:, pc2 * 2048:(pc2 + 1) * 2048])
            hi_bf = AFV(18432, 1024).bitcast(BF16)
            w("scalar", "din")
            inc("scalar", "s", 1, lambda e, hi=hi_bf, st=stage: e.activation(hi, st, AF.Copy))
            w("vector", "s")
            inc("vector", "v", 1, lambda e, lo=lo_bf, st=stage, hi=hi_bf: e.tensor_tensor(
                lo, st, hi, ALU.subtract))
            w("gpsimd", "v")
            dma("gpsimd", "dst", weh_d[e_i][:, pc2 * 2048:(pc2 + 1) * 2048], hi_bf)
            dma("gpsimd", "dst", wel_d[e_i][:, pc2 * 2048:(pc2 + 1) * 2048], lo_bf)

    pc_yslot = [(24576, 28672), (32768, 36864)]
    pc_dg = {}
    for s2 in (0, 1):
        par = s2 % 2
        dma("gpsimd", "dg", ABV(pc_yslot[par][0], 4096), yth_d[s2])
        dma("gpsimd", "dg", ABV(pc_yslot[par][1], 4096), ytl_d[s2])
        pc_dg[s2] = dtok("dg")
    for sc in range(NSC):
        par = sc % 2
        ysl = pc_yslot[par]
        w("tensor", "dg", pc_dg[sc])
        wsnap("tensor", wo_done)
        for fh in range(2):
            for wj in range(4):
                tc = sc * 4 + wj
                bi = (fh * 4 + wj) % 2
                bank = pb[bi]
                bank_wait("tensor", bi)

                def mk_o(e, fh=fh, wj=wj, bank=bank, ysl=ysl):
                    last = None
                    for kc in range(8):
                        yh = ABV(ysl[0] + kc * 512 + wj * 128, 128)
                        yl = ABV(ysl[1] + kc * 512 + wj * 128, 128)
                        wh = ABV(8192 + kc * 1024 + fh * 512, 512)
                        wl = ABV(16384 + kc * 1024 + fh * 512, 512)
                        e.matmul(bank[:, 0:512], yh, wh, start=(kc == 0), stop=False)
                        e.matmul(bank[:, 0:512], yh, wl, start=False, stop=False)
                        last = e.matmul(bank[:, 0:512], yl, wh, start=False, stop=(kc == 7))
                    return last
                inc("tensor", "t", 1, mk_o)
                w("vector", "t")
                dst = AFV(tc * 1024 + fh * 512, 512)
                inc("vector", "v", 1, lambda e, dst=dst, bank=bank, fh=fh: e.scalar_tensor_tensor(
                    dst, bank[:, 0:512], 1.0, AFV(20480 + fh * 512, 512),
                    ALU.mult, ALU.add))
                bank_free[bi] = ("v", cnt["v"])
                emit_presplit(1)
        if sc + 2 < NSC:
            w("gpsimd", "t")
            dma("gpsimd", "dg", ABV(pc_yslot[par][0], 4096), yth_d[sc + 2])
            dma("gpsimd", "dg", ABV(pc_yslot[par][1], 4096), ytl_d[sc + 2])
            pc_dg[sc + 2] = dtok("dg")
    emit_presplit(99)
    w("gpsimd", "dst")
    barrier()

    for sc in range(NSC):
        if sc > 0:
            w("gpsimd", "t")
        dma("gpsimd", "dg", ABV(0, 4096), xnth_d[sc])
        dma("gpsimd", "dg", ABV(4096, 4096), xntl_d[sc])
        w("tensor", "dg")
        bank_wait("tensor", 2)

        def mk_sc(e, sc=sc):
            last = None
            for wj in range(4):
                for kc in range(8):
                    xh = ABV(kc * 512 + wj * 128, 128)
                    xl = ABV(4096 + kc * 512 + wj * 128, 128)
                    gh = gwhi[:, kc * 8:(kc + 1) * 8]
                    gl = gwlo[:, kc * 8:(kc + 1) * 8]
                    reg = pb[2][:, wj * 8:(wj + 1) * 8]
                    e.matmul(reg, xh, gh, start=(kc == 0), stop=False)
                    e.matmul(reg, xh, gl, start=False, stop=False)
                    last = e.matmul(reg, xl, gh, start=False, stop=(kc == 7))
            return last
        inc("tensor", "t", 1, mk_sc)
        w("scalar", "t")
        w("scalar", "v")
        inc("scalar", "s", 1, lambda e: e.activation(scores8[:, 0:32], pb[2][:, 0:32], AF.Copy))
        bank_free[2] = ("s", cnt["s"])
        w("vector", "s")
        for wj in range(4):
            tc = sc * 4 + wj
            s8 = scores8[:, wj * 8:(wj + 1) * 8]
            inc("vector", "v", 1, lambda e, s8=s8: e.reduce_max(mx1[:, :], s8, axis=AX.X))
            w("vector", "v")
            inc("vector", "v", 1, lambda e, s8=s8: e.tensor_scalar(
                sc8a[:, :], s8, mx1[:, 0:1], None, ALU.is_ge))
            w("vector", "v")
            inc("vector", "v", 1, lambda e: e.tensor_tensor_scan(
                sc8b[:, :], sc8a[:, :], zero128[:, :], 0.0, ALU.add, ALU.add))
            w("vector", "v")
            inc("vector", "v", 1, lambda e: e.tensor_scalar(
                sc8c[:, :], sc8b[:, :], 1.0, None, ALU.is_equal))
            w("vector", "v")
            inc("vector", "v", 1, lambda e, tc=tc: e.tensor_tensor(
                onehot[:, tc * 8:(tc + 1) * 8], sc8a[:, :], sc8c[:, :], ALU.mult))
    if debug:
        w("gpsimd", "v")
        dma("gpsimd", "o", dbg["d_onehot"], onehot[:, :])
    barrier()
    xslot = [(0, 4096), (24576, 28672)]
    xnT_ready = {}
    hi_slot = [lambda off, wdt: ABV(8192 + off, wdt),
               lambda off, wdt: AFV(16384 + off // 2, (wdt + 1) // 2).bitcast(BF16)]
    wdg_hi = {}
    w("gpsimd", "t")
    dma("gpsimd", "dg", ABV(8192, 8192), weh_d[0])
    wdg_hi[0] = dtok("dg")
    for e_i in range(8):
        w("gpsimd", "t")
        dma("gpsimd", "dg", ABV(16384, 8192), wel_d[e_i])
        wdg = dtok("dg")
        for sc in range(NSC):
            par = sc % 2
            if e_i == 0 and sc < 2:
                dma("gpsimd", "dg", ABV(xslot[par][0], 4096), xnth_d[sc])
                dma("gpsimd", "dg", ABV(xslot[par][1], 4096), xntl_d[sc])
                xnT_ready[(e_i, sc)] = dtok("dg")
            w("tensor", "dg", xnT_ready[(e_i, sc)])
            w("tensor", "dg", wdg)
            for fh in range(2):
                for wj in range(4):
                    tc = sc * 4 + wj
                    bi = (fh * 4 + wj) % 2
                    bank = pb[bi]
                    bank_wait("tensor", bi)

                    def mk_e(e, fh=fh, wj=wj, bank=bank, par=par, e_i=e_i):
                        last = None
                        for kc in range(8):
                            xh = ABV(xslot[par][0] + kc * 512 + wj * 128, 128)
                            xl = ABV(xslot[par][1] + kc * 512 + wj * 128, 128)
                            wh = hi_slot[e_i % 2](kc * 1024 + fh * 512, 512)
                            wl = ABV(16384 + kc * 1024 + fh * 512, 512)
                            e.matmul(bank[:, 0:512], xh, wh, start=(kc == 0), stop=False)
                            e.matmul(bank[:, 0:512], xh, wl, start=False, stop=False)
                            last = e.matmul(bank[:, 0:512], xl, wh, start=False, stop=(kc == 7))
                        return last
                    inc("tensor", "t", 1, mk_e)
                    w("vector", "t")
                    dst = AFV(tc * 1024 + fh * 512, 512)
                    inc("vector", "v", 1, lambda e, dst=dst, bank=bank, tc=tc, e_i=e_i: e.scalar_tensor_tensor(
                        dst, bank[:, 0:512], onehot[:, tc * 8 + e_i:tc * 8 + e_i + 1],
                        dst, ALU.mult, ALU.add))
                    bank_free[bi] = ("v", cnt["v"])
            tdone = cnt["t"]
            nxt = (e_i, sc + 2) if sc + 2 < NSC else (e_i + 1, (sc + 2) % NSC)
            if nxt[0] < 8:
                npar = nxt[1] % 2
                w("gpsimd", "t", tdone)
                dma("gpsimd", "dg", ABV(xslot[npar][0], 4096), xnth_d[nxt[1]])
                dma("gpsimd", "dg", ABV(xslot[npar][1], 4096), xntl_d[nxt[1]])
                xnT_ready[nxt] = dtok("dg")
            if sc == 2 and e_i + 1 < 8:
                hs = hi_slot[(e_i + 1) % 2](0, 8192)
                dma("gpsimd", "dg", hs, weh_d[e_i + 1])
                wdg_hi[e_i + 1] = dtok("dg")
    barrier()

    if debug:
        dma("gpsimd", "o", dbg["d_ASC"], AFV(0, 4096))
    pe_vdone = {}
    for tc in range(NCH):
        xst = AFV(16384 + (tc % 2) * 1024, 1024)
        if tc >= 2:
            w("sync", "v", pe_vdone[tc - 2])
        dma("sync", "din", xst, x_in[tc * 128:(tc + 1) * 128, :])
        w("vector", "din")
        xn8 = AFV(18432, 1024)
        run("vector", lambda e, xst=xst, tc=tc, xn8=xn8: e.tensor_scalar(
            xn8, xst, rstd_a[:, tc:tc + 1], nmr_a[:, tc:tc + 1], ALU.mult, ALU.add))
        dst = AFV(tc * 1024, 1024)
        run("vector", lambda e, dst=dst, xst=xst: e.tensor_tensor(dst, dst, xst, ALU.add))
        w("vector", "v")
        inc("vector", "v", 1, lambda e, dst=dst, xn8=xn8: e.tensor_tensor(dst, dst, xn8, ALU.add))
        pe_vdone[tc] = cnt["v"]
        w("sync", "v", pe_vdone[tc])
        dma("sync", "o", out_ext[tc * 128:(tc + 1) * 128, :], dst)
    w("sync", "o")

    with nc.Block() as block:
        for eng in ENGS:
            def runner(e, lst=plan[eng]):
                for fn in lst:
                    fn(e)
            getattr(block, eng)(runner)
    stack.close()
    return nc


_PERM = np.concatenate([
    np.concatenate([np.arange(h * 128, h * 128 + 64) for h in range(NH)]),
    np.concatenate([np.arange(h * 128 + 64, h * 128 + 128) for h in range(NH)]),
])


def _host_inputs(inputs):
    f32 = np.float32
    x = np.ascontiguousarray(np.asarray(inputs["L0_attn_x"], f32).reshape(L, HID))
    norm_w = np.asarray(inputs["L0_attn_norm_w"], f32)
    Wq = np.asarray(inputs["L0_attn_Wq"], f32)
    Wk = np.asarray(inputs["L0_attn_Wk"], f32)
    Wv = np.asarray(inputs["L0_attn_Wv"], f32)
    Wout = np.asarray(inputs["L0_attn_Wout"], f32)
    bout = np.asarray(inputs["L0_attn_bout"], f32)
    Wbeta = np.asarray(inputs["L0_attn_Wbeta"], f32)
    Wig = np.asarray(inputs["L0_attn_Wig"], f32)
    Wog = np.asarray(inputs["L0_attn_Wog"], f32)
    bg = np.concatenate([np.asarray(inputs["L0_attn_bbeta"], f32),
                         np.asarray(inputs["L0_attn_big"], f32),
                         np.asarray(inputs["L0_attn_bog"], f32)])
    gate_w = np.asarray(inputs["L0_ffn_gate_w"], f32)
    expert_w = np.asarray(inputs["L0_ffn_expert_w"], f32)

    def stat_layout(wt):
        return np.ascontiguousarray(
            wt.reshape(8, 128, 8, 128).transpose(2, 1, 0, 3).reshape(8, 128, 1024))

    def mov_layout(wt):
        return np.ascontiguousarray(
            wt.reshape(8, 128, wt.shape[1]).transpose(1, 0, 2).reshape(128, 8 * wt.shape[1]))

    wq_h = stat_layout(Wq.T[:, _PERM])
    wk_h = stat_layout(Wk.T[:, _PERM])
    wv_h = mov_layout(Wv.T)
    wo_h = mov_layout(Wout.T)
    we_h = np.ascontiguousarray(np.stack([mov_layout(expert_w[e].T) for e in range(8)]))
    wg_h = mov_layout(np.concatenate([Wbeta, Wig, Wog], axis=0).T)
    gw_h = mov_layout(gate_w.T)
    nwc_h = np.ascontiguousarray(norm_w.reshape(8, 128).T)
    boutbc = np.ascontiguousarray(np.broadcast_to(bout[None, :], (128, HID)))
    tri_h = np.triu(np.ones((128, 128), f32))
    ident_h = np.eye(128, dtype=f32)
    ones1_h = np.ones((1, 128), f32)

    half = 64
    inv_freq = (1.0 / 10000.0 ** (np.arange(half, dtype=f32) / half)).astype(f32)
    pos = np.arange(L, dtype=f32)
    freqs = pos[:, None] * inv_freq[None, :]
    cos_t = np.cos(freqs).astype(f32)
    sin_t = np.sin(freqs).astype(f32)

    in_maps = []
    for c in range(NCORES):
        sel = np.zeros((8, 1), f32)
        sel[:c, 0] = 1.0
        onems = np.ascontiguousarray(np.broadcast_to(1.0 - sel, (8, 8)))
        selrow = np.zeros((1, 8), f32)
        selrow[0, :7] = sel[:7, 0]
        cslice = slice(c * T, (c + 1) * T)
        cosr = np.ascontiguousarray(np.tile(cos_t[cslice].T, (2, 1)))
        sinr = np.ascontiguousarray(np.tile(sin_t[cslice].T, (2, 1)))
        in_maps.append(dict(
            x=np.ascontiguousarray(x[cslice]),
            wq=wq_h, wk=wk_h, wv=wv_h, wo=wo_h, we=we_h, wg=wg_h, gw=gw_h,
            bg=np.ascontiguousarray(bg.reshape(24, 1)),
            boutbc=boutbc, nwc=nwc_h, cosr=cosr, sinr=sinr,
            tri=tri_h, ident=ident_h, ones1=ones1_h,
            selc=sel, onems=onems, selrow=selrow,
            epsc=np.ascontiguousarray(np.broadcast_to(
                np.array([1.0 / 1024.0, 1e-5, 1e-6, 0.0], f32)[None, :], (128, 4))),
            esel=np.ascontiguousarray(
                np.repeat(np.eye(16, dtype=f32), 128, axis=1)),
        ))
    return in_maps


def kernel(**inputs):
    in_maps = _host_inputs(inputs)
    nc = build_program()
    res = run_bass_kernel_spmd(nc, in_maps, core_ids=list(range(NCORES)))
    out = np.concatenate([np.asarray(res.results[c]["out"]) for c in range(NCORES)],
                         axis=0).reshape(1, L, HID).astype(np.float32)
    return out
